# version 02
# baseline (speedup 1.0000x reference)
import numpy as np
import os
import ml_dtypes
import concourse.bass as bass
import concourse.mybir as mybir

F32 = mybir.dt.float32
BF16 = mybir.dt.bfloat16
I32 = mybir.dt.int32
U32 = mybir.dt.uint32
AF = mybir.ActivationFunctionType
ALU = mybir.AluOpType
AX = mybir.AxisListType

ENGS = ("sync", "tensor", "vector", "scalar", "gpsimd")
DMA_POOL = 12

SERIALIZE = bool(os.environ.get("MK_SERIALIZE"))


class P:
    def __init__(self, nc):
        self.nc = nc
        self.steps = {e: [] for e in ENGS}
        self.count = {}
        self.ndma = {e: 0 for e in ENGS}
        self.last_w = {}
        self.readers = {}
        self.all_sems = []

    def _key(self, k):
        if k not in self.count:
            self.count[k] = 0
            self.all_sems.append(k)
        return k

    def step(self, eng, fn, reads=(), writes=(), dma=False):
        deps = {}
        def add(tok):
            if tok is None:
                return
            k2, c2 = tok
            deps[k2] = max(deps.get(k2, 0), c2)
        for b in reads:
            add(self.last_w.get(b))
        for b in writes:
            add(self.last_w.get(b))
            for t in self.readers.get(b, ()):
                add(t)
        if SERIALIZE:
            for k2, c2 in self.count.items():
                if c2:
                    deps[k2] = c2
        if dma:
            key = self._key(f"d:{eng}:{self.ndma[eng] % DMA_POOL}")
            self.ndma[eng] += 1
            inc = 16
        else:
            key = self._key(f"c:{eng}")
            inc = 1
        self.count[key] += inc
        tok = (key, self.count[key])
        self.steps[eng].append((fn, deps, inc, key))
        for b in writes:
            self.last_w[b] = tok
            self.readers[b] = []
        for b in reads:
            self.readers.setdefault(b, []).append(tok)
        return tok

    def dma(self, eng, out, in_, reads=(), writes=(), **kw):
        return self.step(eng, lambda E: E.dma_start(out=out, in_=in_, **kw),
                         reads=reads, writes=writes, dma=True)

    def build(self, stack):
        nc = self.nc
        sems = {k: stack.enter_context(nc.semaphore(k.replace(":", "_")))
                for k in self.all_sems}
        blk = stack.enter_context(nc.Block())
        final = dict(self.count)

        def mk_emit(eng):
            def emit(E):
                water = {k: 0 for k in self.all_sems}
                for fn, deps, inc, key in self.steps[eng]:
                    for k2, c2 in deps.items():
                        if c2 > water[k2]:
                            E.wait_ge(sems[k2], c2)
                            water[k2] = c2
                    last = fn(E)
                    assert last is not None, f"step on {eng} returned None"
                    last.then_inc(sems[key], inc)
                for k2, c2 in final.items():
                    if c2 > water[k2]:
                        E.wait_ge(sems[k2], c2)
                        water[k2] = c2
            return emit

        for eng in ENGS:
            if self.steps[eng] or eng == "sync":
                getattr(blk, eng)(mk_emit(eng))


def make_nc():
    return bass.Bass("TRN2", target_bir_lowering=False)


def cdiv(a, b):
    return (a + b - 1) // b


"""Per-launch Bass builders + host staging for the v0 multi-launch kernel.
All layouts are per-core, feature-major [features on partitions, tokens free].
"""
from contextlib import ExitStack

TPC = 2048
LEAD = 4
TT = TPC + LEAD
DM = 2048
DI = 4096
DS = 16
DTR = 128
EPS = 1e-5

NT_FULL = [(i * 512, 512) for i in range(4)]
NT_LEAD = NT_FULL + [(2048, 4)]


def build_A():
    nc = make_nc()
    xs = nc.dram_tensor("xs", [DM, TT], F32, kind="ExternalInput").ap()
    w_in_t = nc.dram_tensor("w_in_t", [DM, 2 * DI], F32, kind="ExternalInput").ap()
    conv_w = nc.dram_tensor("conv_w", [128, 32 * 4], F32, kind="ExternalInput").ap()
    conv_b = nc.dram_tensor("conv_b", [128, 32], F32, kind="ExternalInput").ap()
    w_x_t = nc.dram_tensor("w_x_t", [DI, 160], F32, kind="ExternalInput").ap()
    w_dt_t = nc.dram_tensor("w_dt_t", [DTR, DI], F32, kind="ExternalInput").ap()
    b_dt = nc.dram_tensor("b_dt", [128, 32], F32, kind="ExternalInput").ap()

    o_xc = nc.dram_tensor("xc", [DI, TPC], F32, kind="ExternalOutput").ap()
    o_delta = nc.dram_tensor("delta", [DI, TPC], F32, kind="ExternalOutput").ap()
    o_z = nc.dram_tensor("z", [DI, TPC], F32, kind="ExternalOutput").ap()
    o_bc = nc.dram_tensor("bc", [32, TPC], F32, kind="ExternalOutput").ap()
    o_dtot = nc.dram_tensor("dtot", [128, 32], F32, kind="ExternalOutput").ap()

    xcb_d = nc.dram_tensor("xcb_d", [DI, TPC], BF16, kind="Internal").ap()
    rows_d = nc.dram_tensor("rows_d", [2, TT], F32, kind="Internal").ap()

    p = P(nc)
    with ExitStack() as st:
        e = st.enter_context
        xbuf = [e(nc.sbuf_tensor(f"xbuf{i}", [128, TT], F32)) for i in range(2)]
        sqt = e(nc.sbuf_tensor("sqt", [128, TT], F32))
        ones1 = e(nc.sbuf_tensor("ones1", [128, 1], F32))
        consts = e(nc.sbuf_tensor("consts", [128, 4], F32))
        xn_sb = e(nc.sbuf_tensor("xn_sb", [128, 16 * TT], BF16))
        mu_b = e(nc.sbuf_tensor("mu_b", [128, TT], F32))
        rstd_b = e(nc.sbuf_tensor("rstd_b", [128, TT], F32))
        wbuf = [e(nc.sbuf_tensor(f"wbuf{i}", [128, 2048], BF16)) for i in range(2)]
        cw_sb = e(nc.sbuf_tensor("cw_sb", [128, 128], F32))
        cb_sb = e(nc.sbuf_tensor("cb_sb", [128, 32], F32))
        bdt_sb = e(nc.sbuf_tensor("bdt_sb", [128, 32], F32))
        conv_acc = e(nc.sbuf_tensor("conv_acc", [128, TPC], F32))
        xc_sb = e(nc.sbuf_tensor("xc_sb", [128, TPC], F32))
        xcb_sb = e(nc.sbuf_tensor("xcb_sb", [128, TPC], BF16))
        z_sb = e(nc.sbuf_tensor("z_sb", [128, TPC], F32))
        xcbin = [e(nc.sbuf_tensor(f"xcbin{i}", [128, TPC], BF16)) for i in range(2)]
        wx_sb = [e(nc.sbuf_tensor(f"wx_sb{i}", [128, 160], BF16)) for i in range(2)]
        wdt_sb = [e(nc.sbuf_tensor(f"wdt_sb{i}", [128, 128], BF16)) for i in range(2)]
        dt_bf = e(nc.sbuf_tensor("dt_bf", [128, TPC], BF16))
        bc_sb = e(nc.sbuf_tensor("bc_sb", [32, TPC], F32))
        sp1 = e(nc.sbuf_tensor("sp1", [128, TPC], F32))
        sp2 = e(nc.sbuf_tensor("sp2", [128, TPC], F32))
        delta_sb = e(nc.sbuf_tensor("delta_sb", [128, TPC], F32))
        dtot_sb = e(nc.sbuf_tensor("dtot_sb", [128, 32], F32))
        ps_main = e(nc.psum_tensor("ps_main", [128, TT], F32))

        p.step("vector", lambda E: E.memset(ones1[:], 1.0), writes=("ones1",))
        p.step("vector", lambda E: E.memset(consts[:, 0:1], EPS), writes=("consts",))
        p.step("vector", lambda E: E.memset(consts[:, 1:2], 1.0), writes=("consts",))
        p.step("vector", lambda E: E.memset(consts[:, 2:3], -1.0), writes=("consts",))
        p.dma("sync", cw_sb[:], conv_w[:], writes=("cw",))
        p.dma("sync", cb_sb[:], conv_b[:], writes=("cb",))
        p.dma("sync", bdt_sb[:], b_dt[:], writes=("bdt",))

        for dc in range(16):
            b = dc % 2
            p.dma("sync", xbuf[b][:], xs[dc * 128:(dc + 1) * 128, :],
                  writes=(f"xbuf{b}",))
            def mmrow(E, dc=dc, b=b):
                last = None
                for (o, n) in NT_LEAD:
                    last = E.matmul(ps_main[0:1, o:o + n], ones1[:], xbuf[b][:, o:o + n],
                                    start=(dc == 0), stop=(dc == 15))
                return last
            p.step("tensor", mmrow, reads=(f"xbuf{b}", "ones1"), writes=("ps_main",))
        p.step("vector", lambda E: E.tensor_scalar_mul(mu_b[0:1, :], ps_main[0:1, :], 1.0 / DM),
               reads=("ps_main",), writes=("murow",))
        for dc in range(16):
            b = dc % 2
            p.dma("sync", xbuf[b][:], xs[dc * 128:(dc + 1) * 128, :],
                  writes=(f"xbuf{b}",))
            p.step("scalar", lambda E, b=b: E.activation(out=sqt[:], in_=xbuf[b][:], func=AF.Square),
                   reads=(f"xbuf{b}",), writes=("sqt",))
            def mmrow2(E, dc=dc, b=b):
                last = None
                for (o, n) in NT_LEAD:
                    last = E.matmul(ps_main[0:1, o:o + n], ones1[:], sqt[:, o:o + n],
                                    start=(dc == 0), stop=(dc == 15))
                return last
            p.step("tensor", mmrow2, reads=("sqt", "ones1"), writes=("ps_main",))
        p.step("vector", lambda E: E.tensor_scalar_mul(rstd_b[0:1, :], ps_main[0:1, :], 1.0 / DM),
               reads=("ps_main",), writes=("varow",))
        p.step("vector", lambda E: E.tensor_mul(sqt[0:1, :], mu_b[0:1, :], mu_b[0:1, :]),
               reads=("murow",), writes=("sqrow",))
        p.step("vector", lambda E: E.tensor_sub(rstd_b[0:1, :], rstd_b[0:1, :], sqt[0:1, :]),
               reads=("sqrow", "varow"), writes=("varow",))
        p.step("scalar", lambda E: E.activation(out=rstd_b[0:1, :], in_=rstd_b[0:1, :], func=AF.Sqrt, bias=consts[0:1, 0:1]),
               reads=("varow", "consts"), writes=("varow",))
        p.step("vector", lambda E: E.reciprocal(rstd_b[0:1, :], rstd_b[0:1, :]),
               reads=("varow",), writes=("varow",))
        p.dma("gpsimd", rows_d[0:1, :], mu_b[0:1, :], reads=("murow",), writes=("rows_d",))
        p.dma("gpsimd", rows_d[1:2, :], rstd_b[0:1, :], reads=("varow",), writes=("rows_d",))
        p.dma("sync", mu_b[:], rows_d[0:1, :].partition_broadcast(128),
              reads=("rows_d",), writes=("mu_b",))
        p.dma("sync", rstd_b[:], rows_d[1:2, :].partition_broadcast(128),
              reads=("rows_d",), writes=("rstd_b",))
        for dc in range(16):
            b = dc % 2
            p.dma("sync", xbuf[b][:], xs[dc * 128:(dc + 1) * 128, :],
                  writes=(f"xbuf{b}",))
            p.step("vector", lambda E, b=b: E.tensor_sub(sqt[:], xbuf[b][:], mu_b[:]),
                   reads=(f"xbuf{b}", "mu_b"), writes=("sqt",))
            p.step("vector", lambda E, dc=dc: E.tensor_mul(
                xn_sb[:, dc * TT:(dc + 1) * TT], sqt[:], rstd_b[:]),
                   reads=("sqt", "rstd_b"), writes=("xn",))

        for half, lead in ((0, True), (1, False)):
            for c in range(32):
                b = c % 2
                ch0 = half * DI + c * 128
                src = bass.AP(w_in_t.tensor, ch0 + 0, [[2 * DI, 128], [128 * 2 * DI, 16], [1, 128]])
                p.dma("gpsimd", wbuf[b][:].rearrange("p (a j) -> p a j", a=16), src,
                      writes=(f"wbuf{b}",))
                nts = NT_LEAD if lead else NT_FULL
                def mmxz(E, b=b, nts=nts, lead=lead):
                    last = None
                    for dc in range(16):
                        for (o, n) in nts:
                            ro = o if lead else o + LEAD
                            last = E.matmul(ps_main[:, o:o + n],
                                            wbuf[b][:, dc * 128:(dc + 1) * 128],
                                            xn_sb[:, dc * TT + ro: dc * TT + ro + n],
                                            start=(dc == 0), stop=(dc == 15))
                    return last
                p.step("tensor", mmxz, reads=(f"wbuf{b}", "xn"), writes=("ps_main",))
                if lead:
                    def conv0(E, c=c):
                        return E.tensor_scalar(conv_acc[:], ps_main[:, 1:1 + TPC],
                                               cw_sb[:, c * 4:c * 4 + 1],
                                               cb_sb[:, c:c + 1],
                                               op0=ALU.mult, op1=ALU.add)
                    p.step("vector", conv0, reads=("ps_main", "cw", "cb"), writes=("conv_acc",))
                    for k in range(1, 4):
                        def convk(E, c=c, k=k):
                            return E.scalar_tensor_tensor(
                                conv_acc[:], ps_main[:, 1 + k:1 + k + TPC],
                                cw_sb[:, c * 4 + k:c * 4 + k + 1], conv_acc[:],
                                op0=ALU.mult, op1=ALU.add)
                        p.step("vector", convk, reads=("ps_main", "cw", "conv_acc"),
                               writes=("conv_acc",))
                    p.step("scalar", lambda E: E.activation(out=xc_sb[:], in_=conv_acc[:], func=AF.Silu),
                           reads=("conv_acc",), writes=("xc_sb",))
                    p.dma("gpsimd", o_xc[c * 128:(c + 1) * 128, :], xc_sb[:],
                          reads=("xc_sb",), writes=("o_xc",))
                    p.step("vector", lambda E: E.tensor_copy(xcb_sb[:], xc_sb[:]),
                           reads=("xc_sb",), writes=("xcb_sb",))
                    p.dma("gpsimd", xcb_d[c * 128:(c + 1) * 128, :], xcb_sb[:],
                          reads=("xcb_sb",), writes=("xcb_d",))
                else:
                    p.step("scalar", lambda E: E.copy(out=z_sb[:], in_=ps_main[:, 0:TPC]),
                           reads=("ps_main",), writes=("z_sb",))
                    p.dma("gpsimd", o_z[c * 128:(c + 1) * 128, :], z_sb[:],
                          reads=("z_sb",), writes=("o_z",))

        for c in range(32):
            b = c % 2
            p.dma("sync", xcbin[b][:], xcb_d[c * 128:(c + 1) * 128, :],
                  reads=("xcb_d",), writes=(f"xcbin{b}",))
            p.dma("gpsimd", wx_sb[b][:], w_x_t[c * 128:(c + 1) * 128, :],
                  writes=(f"wx{b}",))
            def mmdt(E, b=b, c=c):
                last = None
                for (o, n) in NT_FULL:
                    last = E.matmul(ps_main[:, o:o + n], wx_sb[b][:, 0:128],
                                    xcbin[b][:, o:o + n], start=(c == 0), stop=(c == 31))
                return last
            p.step("tensor", mmdt, reads=(f"xcbin{b}", f"wx{b}"), writes=("ps_main",))
        p.step("vector", lambda E: E.tensor_copy(dt_bf[:], ps_main[:, 0:TPC]),
               reads=("ps_main",), writes=("dt_bf",))
        for c in range(32):
            b = c % 2
            p.dma("sync", xcbin[b][:], xcb_d[c * 128:(c + 1) * 128, :],
                  reads=("xcb_d",), writes=(f"xcbin{b}",))
            p.dma("gpsimd", wx_sb[b][:], w_x_t[c * 128:(c + 1) * 128, :],
                  writes=(f"wx{b}",))
            def mmbc(E, b=b, c=c):
                last = None
                for (o, n) in NT_FULL:
                    last = E.matmul(ps_main[0:32, o:o + n], wx_sb[b][:, 128:160],
                                    xcbin[b][:, o:o + n], start=(c == 0), stop=(c == 31))
                return last
            p.step("tensor", mmbc, reads=(f"xcbin{b}", f"wx{b}"), writes=("ps_main",))
        p.step("vector", lambda E: E.tensor_copy(bc_sb[:], ps_main[0:32, 0:TPC]),
               reads=("ps_main",), writes=("bc_sb",))
        p.dma("gpsimd", o_bc[:], bc_sb[:], reads=("bc_sb",), writes=("o_bc",))

        for c in range(32):
            b = c % 2
            p.dma("gpsimd", wdt_sb[b][:], w_dt_t[:, c * 128:(c + 1) * 128],
                  writes=(f"wdt{b}",))
            def mmd(E, b=b):
                last = None
                for (o, n) in NT_FULL:
                    last = E.matmul(ps_main[:, o:o + n], wdt_sb[b][:],
                                    dt_bf[:, o:o + n], start=True, stop=True)
                return last
            p.step("tensor", mmd, reads=(f"wdt{b}", "dt_bf"), writes=("ps_main",))
            p.step("scalar", lambda E, c=c: E.activation(out=sp1[:], in_=ps_main[:, 0:TPC],
                                                         func=AF.Relu, bias=bdt_sb[:, c:c + 1]),
                   reads=("ps_main", "bdt"), writes=("sp1",))
            p.step("scalar", lambda E, c=c: E.activation(out=sp2[:], in_=ps_main[:, 0:TPC],
                                                         func=AF.Abs, bias=bdt_sb[:, c:c + 1]),
                   reads=("ps_main", "bdt"), writes=("sp2",))
            p.step("scalar", lambda E: E.activation(out=sp2[:], in_=sp2[:], func=AF.Exp, scale=consts[:, 2:3]),
                   reads=("sp2", "consts"), writes=("sp2",))
            p.step("scalar", lambda E: E.activation(out=sp2[:], in_=sp2[:], func=AF.Ln, bias=consts[:, 1:2]),
                   reads=("sp2", "consts"), writes=("sp2",))
            p.step("vector", lambda E: E.tensor_add(delta_sb[:], sp1[:], sp2[:]),
                   reads=("sp1", "sp2"), writes=("delta_sb",))
            p.dma("gpsimd", o_delta[c * 128:(c + 1) * 128, :], delta_sb[:],
                  reads=("delta_sb",), writes=("o_delta",))
            p.step("vector", lambda E, c=c: E.reduce_sum(dtot_sb[:, c:c + 1], delta_sb[:], axis=AX.X),
                   reads=("delta_sb",), writes=("dtot",))
        p.dma("gpsimd", o_dtot[:], dtot_sb[:], reads=("dtot",), writes=("o_dtot",))

        p.build(st)
    return nc


def stage_A(inputs, core):
    x = inputs["L0_attn_x"]
    b, q = core // 4, core % 4
    t0 = q * TPC
    xs = np.zeros((TT, DM), np.float32)
    xs[LEAD:] = x[b, t0:t0 + TPC]
    if q > 0:
        xs[1:LEAD] = x[b, t0 - 3:t0]
    w_in = inputs["L0_attn_W_in"]
    conv_w = inputs["L0_attn_conv_w"].reshape(32, 128, 4).transpose(1, 0, 2).reshape(128, 128)
    conv_b = inputs["L0_attn_conv_b"].reshape(32, 128).T.copy()
    b_dt = inputs["L0_attn_b_dt"].reshape(32, 128).T.copy()
    return {
        "xs": np.ascontiguousarray(xs.T),
        "w_in_t": np.ascontiguousarray(w_in.T),
        "conv_w": np.ascontiguousarray(conv_w),
        "conv_b": np.ascontiguousarray(conv_b),
        "w_x_t": np.ascontiguousarray(inputs["L0_attn_W_x"].T),
        "w_dt_t": np.ascontiguousarray(inputs["L0_attn_W_dt"].T),
        "b_dt": np.ascontiguousarray(b_dt),
    }


TB = 512
NTB = TPC // TB

def build_B1():
    nc = make_nc()
    i_xc = nc.dram_tensor("xc", [DI, TPC], F32, kind="ExternalInput").ap()
    i_delta = nc.dram_tensor("delta", [DI, TPC], F32, kind="ExternalInput").ap()
    i_bc = nc.dram_tensor("bc", [32, TPC], F32, kind="ExternalInput").ap()
    i_alog = nc.dram_tensor("alog", [128, 512], F32, kind="ExternalInput").ap()
    o_hend = nc.dram_tensor("hend", [128, 512], F32, kind="ExternalOutput").ap()

    p = P(nc)
    with ExitStack() as st:
        e = st.enter_context
        A_sb = e(nc.sbuf_tensor("A_sb", [128, 512], F32))
        Bb = [e(nc.sbuf_tensor(f"Bb{s}", [128, TB], F32)) for s in range(16)]
        dl = [e(nc.sbuf_tensor(f"dl{i}", [128, TB], F32)) for i in range(2)]
        xct = [e(nc.sbuf_tensor(f"xct{i}", [128, TB], F32)) for i in range(2)]
        ut = [e(nc.sbuf_tensor(f"ut{i}", [128, TB], F32)) for i in range(2)]
        dAt = [e(nc.sbuf_tensor(f"dAt{i}", [128, TB], F32)) for i in range(2)]
        dBt = [e(nc.sbuf_tensor(f"dBt{i}", [128, TB], F32)) for i in range(2)]
        hsc = [e(nc.sbuf_tensor(f"hsc{i}", [128, TB], F32)) for i in range(2)]
        state = e(nc.sbuf_tensor("state", [128, 512], F32))

        p.dma("sync", A_sb[:], i_alog[:], writes=("A",))
        p.step("scalar", lambda E: E.activation(out=A_sb[:], in_=A_sb[:], func=AF.Exp),
               reads=("A",), writes=("A",))
        p.step("vector", lambda E: E.tensor_scalar_mul(A_sb[:], A_sb[:], -1.0),
               reads=("A",), writes=("A",))
        for tb in range(NTB):
            t0 = tb * TB
            for s in range(16):
                p.dma("sync", Bb[s][:], i_bc[s:s + 1, t0:t0 + TB].partition_broadcast(128),
                      writes=(f"Bb{s}",))
            for c in range(32):
                b = c % 2
                p.dma("sync", dl[b][:], i_delta[c * 128:(c + 1) * 128, t0:t0 + TB],
                      writes=(f"dl{b}",))
                p.dma("sync", xct[b][:], i_xc[c * 128:(c + 1) * 128, t0:t0 + TB],
                      writes=(f"xct{b}",))
                p.step("vector", lambda E, b=b: E.tensor_mul(ut[b][:], dl[b][:], xct[b][:]),
                       reads=(f"dl{b}", f"xct{b}"), writes=(f"ut{b}",))
                for s in range(16):
                    i2 = s % 2
                    cs = c * 16 + s
                    p.step("scalar", lambda E, b=b, i2=i2, cs=cs: E.activation(
                        out=dAt[i2][:], in_=dl[b][:], func=AF.Exp, scale=A_sb[:, cs:cs + 1]),
                           reads=(f"dl{b}", "A"), writes=(f"dA{i2}",))
                    p.step("vector", lambda E, b=b, i2=i2, s=s: E.tensor_mul(
                        dBt[i2][:], ut[b][:], Bb[s][:]),
                           reads=(f"ut{b}", f"Bb{s}"), writes=(f"dB{i2}",))
                    if tb == 0:
                        p.step("vector", lambda E, i2=i2: E.tensor_tensor_scan(
                            out=hsc[i2][:], data0=dAt[i2][:], data1=dBt[i2][:],
                            initial=0.0, op0=ALU.mult, op1=ALU.add),
                               reads=(f"dA{i2}", f"dB{i2}"), writes=(f"hsc{i2}",))
                    else:
                        p.step("vector", lambda E, i2=i2, cs=cs: E.tensor_tensor_scan(
                            out=hsc[i2][:], data0=dAt[i2][:], data1=dBt[i2][:],
                            initial=state[:, cs:cs + 1], op0=ALU.mult, op1=ALU.add),
                               reads=(f"dA{i2}", f"dB{i2}", "state"), writes=(f"hsc{i2}",))
                    p.step("vector", lambda E, i2=i2, cs=cs: E.tensor_copy(
                        state[:, cs:cs + 1], hsc[i2][:, TB - 1:TB]),
                           reads=(f"hsc{i2}",), writes=("state",))
        p.dma("gpsimd", o_hend[:], state[:], reads=("state",), writes=("o",))
        p.build(st)
    return nc


def stage_B1(inputs, core, prev):
    alog = inputs["L0_attn_A_log"].reshape(32, 128, 16).transpose(1, 0, 2).reshape(128, 512)
    return {"xc": prev["xc"][core], "delta": prev["delta"][core],
            "bc": prev["bc"][core], "alog": np.ascontiguousarray(alog)}


def build_B2():
    nc = make_nc()
    i_xc = nc.dram_tensor("xc", [DI, TPC], F32, kind="ExternalInput").ap()
    i_delta = nc.dram_tensor("delta", [DI, TPC], F32, kind="ExternalInput").ap()
    i_z = nc.dram_tensor("z", [DI, TPC], F32, kind="ExternalInput").ap()
    i_bc = nc.dram_tensor("bc", [32, TPC], F32, kind="ExternalInput").ap()
    i_alog = nc.dram_tensor("alog", [128, 512], F32, kind="ExternalInput").ap()
    i_dp = nc.dram_tensor("dp", [128, 32], F32, kind="ExternalInput").ap()
    i_hend = nc.dram_tensor("hend_all", [128, 8 * 512], F32, kind="ExternalInput").ap()
    i_dtot = nc.dram_tensor("dtot_all", [128, 8 * 32], F32, kind="ExternalInput").ap()
    i_mask = nc.dram_tensor("mask", [128, 8], F32, kind="ExternalInput").ap()
    i_wout = nc.dram_tensor("w_out_t", [DI, DM], F32, kind="ExternalInput").ap()
    i_x = nc.dram_tensor("x_fm", [DM, TPC], F32, kind="ExternalInput").ap()
    o_x1 = nc.dram_tensor("x1", [DM, TPC], F32, kind="ExternalOutput").ap()
    y_d = nc.dram_tensor("y_d", [DI, TPC], BF16, kind="Internal").ap()

    p = P(nc)
    with ExitStack() as st:
        e = st.enter_context
        A_sb = e(nc.sbuf_tensor("A_sb", [128, 512], F32))
        Bb = [e(nc.sbuf_tensor(f"Bb{s}", [128, TB], F32)) for s in range(16)]
        Cb = [e(nc.sbuf_tensor(f"Cb{s}", [128, TB], F32)) for s in range(16)]
        dl = [e(nc.sbuf_tensor(f"dl{i}", [128, TB], F32)) for i in range(2)]
        xct = [e(nc.sbuf_tensor(f"xct{i}", [128, TB], F32)) for i in range(2)]
        zt = [e(nc.sbuf_tensor(f"zt{i}", [128, TB], F32)) for i in range(2)]
        ut = [e(nc.sbuf_tensor(f"ut{i}", [128, TB], F32)) for i in range(2)]
        dAt = [e(nc.sbuf_tensor(f"dAt{i}", [128, TB], F32)) for i in range(2)]
        dBt = [e(nc.sbuf_tensor(f"dBt{i}", [128, TB], F32)) for i in range(2)]
        hsc = [e(nc.sbuf_tensor(f"hsc{i}", [128, TB], F32)) for i in range(2)]
        hC = e(nc.sbuf_tensor("hC", [128, TB], F32))
        yacc = e(nc.sbuf_tensor("yacc", [128, TB], F32))
        ybf = e(nc.sbuf_tensor("ybf", [128, TB], BF16))
        state = e(nc.sbuf_tensor("state", [128, 512], F32))
        h_in = e(nc.sbuf_tensor("h_in", [128, 512], F32))
        et = e(nc.sbuf_tensor("et", [128, 512], F32))
        dtm = e(nc.sbuf_tensor("dtm", [128, 32], F32))
        mask_sb = e(nc.sbuf_tensor("mask_sb", [128, 8], F32))
        dp_sb = e(nc.sbuf_tensor("dp_sb", [128, 32], F32))
        hend_sb = e(nc.sbuf_tensor("hend_sb", [128, 8 * 512], F32))
        dtot_sb = e(nc.sbuf_tensor("dtot_sb", [128, 8 * 32], F32))
        wo = [e(nc.sbuf_tensor(f"wo{i}", [128, DI], BF16)) for i in range(8)]
        yin = [e(nc.sbuf_tensor(f"yin{i}", [128, TB], BF16)) for i in range(2)]
        xin = e(nc.sbuf_tensor("xin", [128, TB], F32))
        x1t = e(nc.sbuf_tensor("x1t", [128, TB], F32))
        ps_y = [e(nc.psum_tensor(f"ps_y{i}", [128, TB], F32)) for i in range(8)]

        p.dma("sync", A_sb[:], i_alog[:], writes=("A",))
        p.step("scalar", lambda E: E.activation(out=A_sb[:], in_=A_sb[:], func=AF.Exp),
               reads=("A",), writes=("A",))
        p.step("vector", lambda E: E.tensor_scalar_mul(A_sb[:], A_sb[:], -1.0),
               reads=("A",), writes=("A",))
        p.dma("sync", mask_sb[:], i_mask[:], writes=("mask",))
        p.dma("sync", dp_sb[:], i_dp[:], writes=("dp",))
        p.dma("sync", hend_sb[:], i_hend[:], writes=("hend",))
        p.dma("sync", dtot_sb[:], i_dtot[:], writes=("dtot",))
        p.step("vector", lambda E: E.memset(h_in[:], 0.0), writes=("h_in",))
        for q in range(8):
            p.step("vector", lambda E, q=q: E.tensor_scalar_mul(
                dtm[:], dtot_sb[:, q * 32:(q + 1) * 32], mask_sb[:, q:q + 1]),
                   reads=("dtot", "mask"), writes=("dtm",))
            p.step("vector", lambda E: E.tensor_mul(
                et[:].rearrange("p (c s) -> p c s", s=16),
                A_sb[:].rearrange("p (c s) -> p c s", s=16),
                dtm[:].unsqueeze(2).broadcast_to([128, 32, 16])),
                   reads=("A", "dtm"), writes=("et",))
            p.step("scalar", lambda E: E.activation(out=et[:], in_=et[:], func=AF.Exp),
                   reads=("et",), writes=("et",))
            p.step("vector", lambda E: E.tensor_mul(h_in[:], h_in[:], et[:]),
                   reads=("h_in", "et"), writes=("h_in",))
            p.step("vector", lambda E, q=q: E.scalar_tensor_tensor(
                h_in[:], hend_sb[:, q * 512:(q + 1) * 512], mask_sb[:, q:q + 1],
                h_in[:], op0=ALU.mult, op1=ALU.add),
                   reads=("hend", "mask", "h_in"), writes=("h_in",))
        for tb in range(NTB):
            t0 = tb * TB
            for s in range(16):
                p.dma("sync", Bb[s][:], i_bc[s:s + 1, t0:t0 + TB].partition_broadcast(128),
                      writes=(f"Bb{s}",))
                p.dma("sync", Cb[s][:], i_bc[16 + s:17 + s, t0:t0 + TB].partition_broadcast(128),
                      writes=(f"Cb{s}",))
            for c in range(32):
                b = c % 2
                p.dma("sync", dl[b][:], i_delta[c * 128:(c + 1) * 128, t0:t0 + TB],
                      writes=(f"dl{b}",))
                p.dma("sync", xct[b][:], i_xc[c * 128:(c + 1) * 128, t0:t0 + TB],
                      writes=(f"xct{b}",))
                p.dma("sync", zt[b][:], i_z[c * 128:(c + 1) * 128, t0:t0 + TB],
                      writes=(f"zt{b}",))
                p.step("vector", lambda E, b=b: E.tensor_mul(ut[b][:], dl[b][:], xct[b][:]),
                       reads=(f"dl{b}", f"xct{b}"), writes=(f"ut{b}",))
                for s in range(16):
                    i2 = s % 2
                    cs = c * 16 + s
                    p.step("scalar", lambda E, b=b, i2=i2, cs=cs: E.activation(
                        out=dAt[i2][:], in_=dl[b][:], func=AF.Exp, scale=A_sb[:, cs:cs + 1]),
                           reads=(f"dl{b}", "A"), writes=(f"dA{i2}",))
                    p.step("vector", lambda E, b=b, i2=i2, s=s: E.tensor_mul(
                        dBt[i2][:], ut[b][:], Bb[s][:]),
                           reads=(f"ut{b}", f"Bb{s}"), writes=(f"dB{i2}",))
                    init = h_in if tb == 0 else state
                    ikey = "h_in" if tb == 0 else "state"
                    p.step("vector", lambda E, i2=i2, cs=cs, init=init: E.tensor_tensor_scan(
                        out=hsc[i2][:], data0=dAt[i2][:], data1=dBt[i2][:],
                        initial=init[:, cs:cs + 1], op0=ALU.mult, op1=ALU.add),
                           reads=(f"dA{i2}", f"dB{i2}", ikey), writes=(f"hsc{i2}",))
                    p.step("vector", lambda E, i2=i2, cs=cs: E.tensor_copy(
                        state[:, cs:cs + 1], hsc[i2][:, TB - 1:TB]),
                           reads=(f"hsc{i2}",), writes=("state",))
                    p.step("vector", lambda E, i2=i2, s=s: E.tensor_mul(
                        hC[:], hsc[i2][:], Cb[s][:]),
                           reads=(f"hsc{i2}", f"Cb{s}"), writes=("hC",))
                    if s == 0:
                        p.step("vector", lambda E: E.tensor_copy(yacc[:], hC[:]),
                               reads=("hC",), writes=("yacc",))
                    else:
                        p.step("vector", lambda E: E.tensor_add(yacc[:], yacc[:], hC[:]),
                               reads=("hC", "yacc"), writes=("yacc",))
                p.step("vector", lambda E, b=b, c=c: E.scalar_tensor_tensor(
                    yacc[:], xct[b][:], dp_sb[:, c:c + 1], yacc[:],
                    op0=ALU.mult, op1=ALU.add),
                       reads=(f"xct{b}", "dp", "yacc"), writes=("yacc",))
                p.step("scalar", lambda E, b=b: E.activation(out=zt[b][:], in_=zt[b][:], func=AF.Silu),
                       reads=(f"zt{b}",), writes=(f"zt{b}",))
                p.step("vector", lambda E, b=b: E.tensor_mul(ybf[:], yacc[:], zt[b][:]),
                       reads=("yacc", f"zt{b}"), writes=("ybf",))
                p.dma("gpsimd", y_d[c * 128:(c + 1) * 128, t0:t0 + TB], ybf[:],
                      reads=("ybf",), writes=("y_d",))
        for grp in range(2):
            for dm in range(8):
                dmg = grp * 8 + dm
                src = bass.AP(i_wout.tensor, dmg * 128, [[DM, 128], [128 * DM, 32], [1, 128]])
                p.dma("gpsimd", wo[dm][:].rearrange("p (a j) -> p a j", a=32), src,
                      writes=(f"wo{dm}",))
            for tb in range(NTB):
                t0 = tb * TB
                for c in range(32):
                    b = c % 2
                    p.dma("sync", yin[b][:], y_d[c * 128:(c + 1) * 128, t0:t0 + TB],
                          reads=("y_d",), writes=(f"yin{b}",))
                    def mmout(E, b=b, c=c):
                        last = None
                        for dm in range(8):
                            last = E.matmul(ps_y[dm][:], wo[dm][:, c * 128:(c + 1) * 128],
                                            yin[b][:], start=(c == 0), stop=(c == 31))
                        return last
                    p.step("tensor", mmout,
                           reads=(f"yin{b}",) + tuple(f"wo{dm}" for dm in range(8)),
                           writes=tuple(f"ps{dm}" for dm in range(8)))
                for dm in range(8):
                    dmg = grp * 8 + dm
                    p.dma("sync", xin[:], i_x[dmg * 128:(dmg + 1) * 128, t0:t0 + TB],
                          writes=("xin",))
                    p.step("vector", lambda E, dm=dm: E.tensor_add(x1t[:], ps_y[dm][:], xin[:]),
                           reads=(f"ps{dm}", "xin"), writes=("x1t",))
                    p.dma("gpsimd", o_x1[dmg * 128:(dmg + 1) * 128, t0:t0 + TB], x1t[:],
                          reads=("x1t",), writes=("o_x1",))
        p.build(st)
    return nc


def chain_mask(core):
    m = np.zeros(8, np.float32)
    b, q = core // 4, core % 4
    for pq in range(q):
        m[b * 4 + pq] = 1.0
    return m


def stage_B2(inputs, core, prev):
    alog = inputs["L0_attn_A_log"].reshape(32, 128, 16).transpose(1, 0, 2).reshape(128, 512)
    dp = inputs["L0_attn_Dp"].reshape(32, 128).T
    hend_all = np.concatenate([prev["hend"][c] for c in range(8)], axis=1)
    dtot_all = np.concatenate([prev["dtot"][c] for c in range(8)], axis=1)
    mask = np.broadcast_to(chain_mask(core), (128, 8))
    b, q = core // 4, core % 4
    x = inputs["L0_attn_x"][b, q * TPC:(q + 1) * TPC]
    return {"xc": prev["xc"][core], "delta": prev["delta"][core], "z": prev["z"][core],
            "bc": prev["bc"][core], "alog": np.ascontiguousarray(alog),
            "dp": np.ascontiguousarray(dp),
            "hend_all": np.ascontiguousarray(hend_all),
            "dtot_all": np.ascontiguousarray(dtot_all),
            "mask": np.ascontiguousarray(mask),
            "w_out_t": np.ascontiguousarray(inputs["L0_attn_W_out"].T),
            "x_fm": np.ascontiguousarray(x.T)}


NE = 64
CAP = 1024
EBLK = 8208
NTAB = 8 * EBLK

def build_C():
    nc = make_nc()
    i_x1 = nc.dram_tensor("x1", [DM, TPC], F32, kind="ExternalInput").ap()
    i_wg = nc.dram_tensor("w_gate", [DM, NE], F32, kind="ExternalInput").ap()
    i_wsu = nc.dram_tensor("w_su", [DM, 1024], F32, kind="ExternalInput").ap()
    i_wsd = nc.dram_tensor("w_sd", [512, DM], F32, kind="ExternalInput").ap()
    i_idbf = nc.dram_tensor("ident_bf", [128, 128], BF16, kind="ExternalInput").ap()
    i_idf = nc.dram_tensor("ident_f", [128, 128], F32, kind="ExternalInput").ap()
    i_iota64 = nc.dram_tensor("iota64", [128, NE], F32, kind="ExternalInput").ap()
    o_xtm = nc.dram_tensor("x2n_tm", [TPC, DM], BF16, kind="ExternalOutput").ap()
    o_sh = nc.dram_tensor("shared", [DM, TPC], F32, kind="ExternalOutput").ap()
    o_eidx = nc.dram_tensor("e_idx", [128, 96], F32, kind="ExternalOutput").ap()
    o_wn = nc.dram_tensor("wnorm", [128, 96], F32, kind="ExternalOutput").ap()
    o_ct = nc.dram_tensor("ctot", [1, NE], F32, kind="ExternalOutput").ap()
    rows_d = nc.dram_tensor("rows_d", [2, TPC], F32, kind="Internal").ap()

    p = P(nc)
    with ExitStack() as st:
        e = st.enter_context
        xbuf = [e(nc.sbuf_tensor(f"xbuf{i}", [128, TPC], F32)) for i in range(2)]
        sqt = e(nc.sbuf_tensor("sqt", [128, TPC], F32))
        ones1 = e(nc.sbuf_tensor("ones1", [128, 1], F32))
        ones_bf = e(nc.sbuf_tensor("ones_bf", [128, 1], BF16))
        consts = e(nc.sbuf_tensor("consts", [128, 4], F32))
        xn_sb = e(nc.sbuf_tensor("xn_sb", [128, 16 * TPC], BF16))
        mu_b = e(nc.sbuf_tensor("mu_b", [128, TPC], F32))
        rstd_b = e(nc.sbuf_tensor("rstd_b", [128, TPC], F32))
        idbf = e(nc.sbuf_tensor("idbf", [128, 128], BF16))
        idf = e(nc.sbuf_tensor("idf", [128, 128], F32))
        iota64 = e(nc.sbuf_tensor("iota64_sb", [128, NE], F32))
        ttile = e(nc.sbuf_tensor("ttile", [128, 16 * 128], BF16))
        wg_sb = e(nc.sbuf_tensor("wg_sb", [128, 16 * NE], BF16))
        sc_fm = e(nc.sbuf_tensor("sc_fm", [64, TPC], F32))
        sc_tm = e(nc.sbuf_tensor("sc_tm", [128, 16 * NE], F32))
        scw = e(nc.sbuf_tensor("scw", [128, 16 * NE], F32))
        vcol = e(nc.sbuf_tensor("vcol", [128, 1], F32))
        icol = e(nc.sbuf_tensor("icol", [128, 1], F32))
        tmp64 = e(nc.sbuf_tensor("tmp64", [128, NE], F32))
        oh = e(nc.sbuf_tensor("oh", [128, NE], F32))
        iotaB = e(nc.sbuf_tensor("iotaB", [128, NE], F32))
        ev = e(nc.sbuf_tensor("ev", [128, 6], F32))
        wv = e(nc.sbuf_tensor("wv", [128, 6], F32))
        wsum = e(nc.sbuf_tensor("wsum", [128, 1], F32))
        m6bf = e(nc.sbuf_tensor("m6bf", [128, NE], BF16))
        wsu_sb = e(nc.sbuf_tensor("wsu_sb", [128, 16 * 1024], BF16))
        wsd_sb = e(nc.sbuf_tensor("wsd_sb", [128, 4 * DM], BF16))
        gs = e(nc.sbuf_tensor("gs", [128, 4 * TPC], BF16))
        sh_sb = e(nc.sbuf_tensor("sh_sb", [128, TPC], F32))
        ps_a = e(nc.psum_tensor("ps_a", [128, 2048], F32))
        ps_t = e(nc.psum_tensor("ps_t", [128, 128], BF16))
        ps_f = e(nc.psum_tensor("ps_f", [128, 128], F32))
        ps_c = e(nc.psum_tensor("ps_c", [64, 1], F32))

        p.step("vector", lambda E: E.memset(ones1[:], 1.0), writes=("ones1",))
        p.step("vector", lambda E: E.memset(ones_bf[:], 1.0), writes=("ones_bf",))
        p.step("vector", lambda E: E.memset(consts[:, 0:1], EPS), writes=("consts",))
        p.dma("sync", idbf[:], i_idbf[:], writes=("idbf",))
        p.dma("sync", idf[:], i_idf[:], writes=("idf",))
        p.dma("sync", iota64[:], i_iota64[:], writes=("iota64",))
        p.step("vector", lambda E: E.tensor_scalar_add(iotaB[:], iota64[:], 1e4),
               reads=("iota64",), writes=("iotaB",))

        for dc in range(16):
            b = dc % 2
            p.dma("sync", xbuf[b][:], i_x1[dc * 128:(dc + 1) * 128, :], writes=(f"xbuf{b}",))
            def mmrow(E, dc=dc, b=b):
                last = None
                for (o, n) in NT_FULL:
                    last = E.matmul(ps_a[0:1, o:o + n], ones1[:], xbuf[b][:, o:o + n],
                                    start=(dc == 0), stop=(dc == 15))
                return last
            p.step("tensor", mmrow, reads=(f"xbuf{b}", "ones1"), writes=("ps_a",))
        p.step("vector", lambda E: E.tensor_scalar_mul(mu_b[0:1, :], ps_a[0:1, 0:TPC], 1.0 / DM),
               reads=("ps_a",), writes=("murow",))
        for dc in range(16):
            b = dc % 2
            p.dma("sync", xbuf[b][:], i_x1[dc * 128:(dc + 1) * 128, :], writes=(f"xbuf{b}",))
            p.step("scalar", lambda E, b=b: E.activation(out=sqt[:], in_=xbuf[b][:], func=AF.Square),
                   reads=(f"xbuf{b}",), writes=("sqt",))
            def mmrow2(E, dc=dc, b=b):
                last = None
                for (o, n) in NT_FULL:
                    last = E.matmul(ps_a[0:1, o:o + n], ones1[:], sqt[:, o:o + n],
                                    start=(dc == 0), stop=(dc == 15))
                return last
            p.step("tensor", mmrow2, reads=("sqt", "ones1"), writes=("ps_a",))
        p.step("vector", lambda E: E.tensor_scalar_mul(rstd_b[0:1, :], ps_a[0:1, 0:TPC], 1.0 / DM),
               reads=("ps_a",), writes=("varow",))
        p.step("vector", lambda E: E.tensor_mul(sqt[0:1, 0:TPC], mu_b[0:1, :], mu_b[0:1, :]),
               reads=("murow",), writes=("sqrow",))
        p.step("vector", lambda E: E.tensor_sub(rstd_b[0:1, :], rstd_b[0:1, :], sqt[0:1, 0:TPC]),
               reads=("sqrow", "varow"), writes=("varow",))
        p.step("scalar", lambda E: E.activation(out=rstd_b[0:1, :], in_=rstd_b[0:1, :],
                                                func=AF.Sqrt, bias=consts[0:1, 0:1]),
               reads=("varow", "consts"), writes=("varow",))
        p.step("vector", lambda E: E.reciprocal(rstd_b[0:1, :], rstd_b[0:1, :]),
               reads=("varow",), writes=("varow",))
        p.dma("gpsimd", rows_d[0:1, :], mu_b[0:1, :], reads=("murow",), writes=("rows_d",))
        p.dma("gpsimd", rows_d[1:2, :], rstd_b[0:1, :], reads=("varow",), writes=("rows_d",))
        p.dma("sync", mu_b[:], rows_d[0:1, :].partition_broadcast(128),
              reads=("rows_d",), writes=("mu_b",))
        p.dma("sync", rstd_b[:], rows_d[1:2, :].partition_broadcast(128),
              reads=("rows_d",), writes=("rstd_b",))
        for dc in range(16):
            b = dc % 2
            p.dma("sync", xbuf[b][:], i_x1[dc * 128:(dc + 1) * 128, :], writes=(f"xbuf{b}",))
            p.step("vector", lambda E, b=b: E.tensor_sub(sqt[:], xbuf[b][:], mu_b[:]),
                   reads=(f"xbuf{b}", "mu_b"), writes=("sqt",))
            p.step("vector", lambda E, dc=dc: E.tensor_mul(
                xn_sb[:, dc * TPC:(dc + 1) * TPC], sqt[:], rstd_b[:]),
                   reads=("sqt", "rstd_b"), writes=("xn",))

        for t in range(16):
            for dc in range(16):
                p.step("tensor", lambda E, t=t, dc=dc: E.transpose(
                    ps_t[:], xn_sb[:, dc * TPC + t * 128: dc * TPC + (t + 1) * 128], idbf[:]),
                       reads=("xn", "idbf"), writes=("ps_t",))
                p.step("vector", lambda E, dc=dc: E.tensor_copy(
                    ttile[:, dc * 128:(dc + 1) * 128], ps_t[:]),
                       reads=("ps_t",), writes=("ttile",))
            p.dma("gpsimd", o_xtm[t * 128:(t + 1) * 128, :], ttile[:],
                  reads=("ttile",), writes=("o_xtm",))

        p.dma("gpsimd", wg_sb[:].rearrange("p (a j) -> p a j", a=16),
              bass.AP(i_wg.tensor, 0, [[NE, 128], [128 * NE, 16], [1, NE]]),
              writes=("wg",))
        def mmg(E):
            last = None
            for dc in range(16):
                for (o, n) in NT_FULL:
                    last = E.matmul(ps_a[0:64, o:o + n], wg_sb[:, dc * NE:(dc + 1) * NE],
                                    xn_sb[:, dc * TPC + o: dc * TPC + o + n],
                                    start=(dc == 0), stop=(dc == 15))
            return last
        p.step("tensor", mmg, reads=("wg", "xn"), writes=("ps_a",))
        p.step("scalar", lambda E: E.activation(out=sc_fm[:], in_=ps_a[0:64, 0:TPC], func=AF.Sigmoid),
               reads=("ps_a",), writes=("sc_fm",))
        for t in range(16):
            p.step("tensor", lambda E, t=t: E.transpose(
                ps_f[:, 0:64], sc_fm[:, t * 128:(t + 1) * 128], idf[0:64, 0:64]),
                   reads=("sc_fm", "idf"), writes=("ps_f",))
            p.step("vector", lambda E, t=t: E.tensor_copy(
                sc_tm[:, t * NE:(t + 1) * NE], ps_f[:, 0:64]),
                   reads=("ps_f",), writes=("sc_tm",))
        for t in range(16):
            sl = slice(t * NE, (t + 1) * NE)
            p.step("vector", lambda E, sl=sl: E.tensor_copy(scw[:, sl], sc_tm[:, sl]),
                   reads=("sc_tm",), writes=("scw",))
            p.step("vector", lambda E: E.memset(m6bf[:], 0.0), writes=("m6",))
            for k in range(6):
                p.step("vector", lambda E, sl=sl: E.reduce_max(vcol[:], scw[:, sl], axis=AX.X),
                       reads=("scw",), writes=("vcol",))
                p.step("vector", lambda E, sl=sl: E.tensor_scalar(
                    oh[:], scw[:, sl], vcol[:], None, op0=ALU.is_ge),
                       reads=("scw", "vcol"), writes=("oh",))
                p.step("vector", lambda E: E.scalar_tensor_tensor(
                    tmp64[:], oh[:], -1e4, iotaB[:], op0=ALU.mult, op1=ALU.add),
                       reads=("oh", "iotaB"), writes=("tmp64",))
                p.step("vector", lambda E: E.tensor_reduce(icol[:], tmp64[:], axis=AX.X, op=ALU.min),
                       reads=("tmp64",), writes=("icol",))
                p.step("vector", lambda E: E.tensor_scalar(
                    oh[:], iota64[:], icol[:], None, op0=ALU.is_equal),
                       reads=("iota64", "icol"), writes=("oh",))
                p.step("vector", lambda E: E.tensor_add(m6bf[:], m6bf[:], oh[:]),
                       reads=("m6", "oh"), writes=("m6",))
                p.step("vector", lambda E, k=k: E.tensor_copy(ev[:, k:k + 1], icol[:]),
                       reads=("icol",), writes=("ev",))
                p.step("vector", lambda E, k=k: E.tensor_copy(wv[:, k:k + 1], vcol[:]),
                       reads=("vcol",), writes=("wv",))
                p.step("vector", lambda E, sl=sl: E.scalar_tensor_tensor(
                    scw[:, sl], oh[:], -1e9, scw[:, sl], op0=ALU.mult, op1=ALU.add),
                       reads=("oh", "scw"), writes=("scw",))
            p.step("vector", lambda E: E.reduce_sum(wsum[:], wv[:], axis=AX.X),
                   reads=("wv",), writes=("wsum",))
            p.step("vector", lambda E: E.reciprocal(wsum[:], wsum[:]),
                   reads=("wsum",), writes=("wsum",))
            p.step("vector", lambda E: E.tensor_scalar_mul(wv[:], wv[:], wsum[:]),
                   reads=("wv", "wsum"), writes=("wv",))
            p.dma("gpsimd", o_eidx[:, t * 6:(t + 1) * 6], ev[:], reads=("ev",), writes=("o_e",))
            p.dma("gpsimd", o_wn[:, t * 6:(t + 1) * 6], wv[:], reads=("wv",), writes=("o_w",))
            p.step("tensor", lambda E, t=t: E.matmul(ps_c[:], m6bf[:], ones_bf[:],
                                                     start=(t == 0), stop=(t == 15)),
                   reads=("m6", "ones_bf"), writes=("ps_c",))
        p.step("vector", lambda E: E.tensor_copy(sc_tm[0:64, 0:1], ps_c[:]),
               reads=("ps_c",), writes=("ctcol",))
        def mmct(E):
            return E.matmul(ps_f[0:1, 0:64], sc_tm[0:64, 0:1], idf[0:64, 0:64],
                            start=True, stop=True)
        p.step("tensor", mmct, reads=("ctcol", "idf"), writes=("ps_f",))
        p.step("vector", lambda E: E.tensor_copy(sc_tm[0:1, 0:NE], ps_f[0:1, 0:64]),
               reads=("ps_f",), writes=("ctrow",))
        p.dma("gpsimd", o_ct[:], sc_tm[0:1, 0:NE], reads=("ctrow",), writes=("o_ct",))

        p.dma("gpsimd", wsu_sb[:].rearrange("p (a j) -> p a j", a=16),
              bass.AP(i_wsu.tensor, 0, [[1024, 128], [128 * 1024, 16], [1, 1024]]),
              writes=("wsu",))
        for fc in range(8):
            def mmsu(E, fc=fc):
                last = None
                for dc in range(16):
                    for (o, n) in NT_FULL:
                        last = E.matmul(ps_a[:, o:o + n],
                                        wsu_sb[:, dc * 1024 + fc * 128: dc * 1024 + (fc + 1) * 128],
                                        xn_sb[:, dc * TPC + o: dc * TPC + o + n],
                                        start=(dc == 0), stop=(dc == 15))
                return last
            p.step("tensor", mmsu, reads=("wsu", "xn"), writes=("ps_a",))
            if fc < 4:
                p.step("vector", lambda E, fc=fc: E.tensor_copy(gs[:, fc * TPC:(fc + 1) * TPC], ps_a[:, 0:TPC]),
                       reads=("ps_a",), writes=("gs",))
            else:
                p.step("scalar", lambda E: E.activation(out=sqt[:], in_=ps_a[:, 0:TPC], func=AF.Silu),
                       reads=("ps_a",), writes=("sqt",))
                p.step("vector", lambda E, fc=fc: E.tensor_mul(
                    gs[:, (fc - 4) * TPC:(fc - 3) * TPC], gs[:, (fc - 4) * TPC:(fc - 3) * TPC], sqt[:]),
                       reads=("gs", "sqt"), writes=("gs",))
        p.dma("gpsimd", wsd_sb[:].rearrange("p (a j) -> p a j", a=4),
              bass.AP(i_wsd.tensor, 0, [[DM, 128], [128 * DM, 4], [1, DM]]),
              writes=("wsd",))
        for dmc in range(16):
            def mmsd(E, dmc=dmc):
                last = None
                for hc in range(4):
                    for (o, n) in NT_FULL:
                        last = E.matmul(ps_a[:, o:o + n],
                                        wsd_sb[:, hc * DM + dmc * 128: hc * DM + (dmc + 1) * 128],
                                        gs[:, hc * TPC + o: hc * TPC + o + n],
                                        start=(hc == 0), stop=(hc == 3))
                return last
            p.step("tensor", mmsd, reads=("wsd", "gs"), writes=("ps_a",))
            p.step("vector", lambda E: E.tensor_copy(sh_sb[:], ps_a[:, 0:TPC]),
                   reads=("ps_a",), writes=("sh_sb",))
            p.dma("gpsimd", o_sh[dmc * 128:(dmc + 1) * 128, :], sh_sb[:],
                  reads=("sh_sb",), writes=("o_sh",))
        p.build(st)
    return nc


def stage_C(inputs, core, x1):
    ident = np.eye(128, dtype=np.float32)
    iota = np.broadcast_to(np.arange(NE, dtype=np.float32), (128, NE))
    return {"x1": x1[core], "w_gate": inputs["L0_ffn_w_gate"],
            "w_su": inputs["L0_ffn_w_shared_up"], "w_sd": inputs["L0_ffn_w_shared_down"],
            "ident_bf": ident.astype(np.float32).astype(ml_dtypes.bfloat16),
            "ident_f": ident, "iota64": np.ascontiguousarray(iota)}


def build_D():
    nc = make_nc()
    i_eidx = nc.dram_tensor("e_idx", [128, 96], F32, kind="ExternalInput").ap()
    i_wn = nc.dram_tensor("wnorm", [128, 96], F32, kind="ExternalInput").ap()
    i_ct = nc.dram_tensor("ctot_t", [NE, 8], F32, kind="ExternalInput").ap()
    i_cm = nc.dram_tensor("cmask", [NE, 8], F32, kind="ExternalInput").ap()
    i_tok = nc.dram_tensor("tokid", [128, 16], F32, kind="ExternalInput").ap()
    i_tris = nc.dram_tensor("tris", [128, 128], BF16, kind="ExternalInput").ap()
    i_iota64 = nc.dram_tensor("iota64", [128, NE], F32, kind="ExternalInput").ap()
    i_eb = nc.dram_tensor("ebase64", [128, NE], F32, kind="ExternalInput").ap()
    i_db = nc.dram_tensor("dump64", [128, NE], F32, kind="ExternalInput").ap()
    o_part = nc.dram_tensor("partial", [NTAB, 2], F32, kind="ExternalOutput").ap()
    brow_d = nc.dram_tensor("brow_d", [1, NE], F32, kind="Internal").ap()

    p = P(nc)
    with ExitStack() as st:
        e = st.enter_context
        eidx = e(nc.sbuf_tensor("eidx", [128, 96], F32))
        wn = e(nc.sbuf_tensor("wn", [128, 96], F32))
        ct_t = e(nc.sbuf_tensor("ct_t", [NE, 8], F32))
        cm = e(nc.sbuf_tensor("cm", [NE, 8], F32))
        tok = e(nc.sbuf_tensor("tok_sb", [128, 16], F32))
        tris = e(nc.sbuf_tensor("tris_sb", [128, 128], BF16))
        iota64 = e(nc.sbuf_tensor("iota64_sb", [128, NE], F32))
        eb64 = e(nc.sbuf_tensor("eb64", [128, NE], F32))
        db64 = e(nc.sbuf_tensor("db64", [128, NE], F32))
        ones_bf = e(nc.sbuf_tensor("ones_bf", [128, 1], BF16))
        basecol = e(nc.sbuf_tensor("basecol", [NE, 1], F32))
        base_b = e(nc.sbuf_tensor("base_b", [128, NE], F32))
        ohk = [e(nc.sbuf_tensor(f"ohk{k}", [128, NE], BF16)) for k in range(6)]
        m6 = e(nc.sbuf_tensor("m6", [128, NE], BF16))
        posT = e(nc.sbuf_tensor("posT", [128, NE], F32))
        tmp64 = e(nc.sbuf_tensor("tmp64", [128, NE], F32))
        pcol = e(nc.sbuf_tensor("pcol", [128, 1], F32))
        ecol = e(nc.sbuf_tensor("ecol", [128, 1], F32))
        emc = e(nc.sbuf_tensor("emc", [128, 1], F32))
        dcol = e(nc.sbuf_tensor("dcol", [128, 1], F32))
        r1 = e(nc.sbuf_tensor("r1", [128, 1], F32))
        rd = e(nc.sbuf_tensor("rd", [128, 1], F32))
        mcol = e(nc.sbuf_tensor("mcol", [128, 1], F32))
        rowf = e(nc.sbuf_tensor("rowf", [128, 1], F32))
        rowi = e(nc.sbuf_tensor("rowi", [128, 1], I32))
        content = e(nc.sbuf_tensor("content", [128, 2], F32))
        ztile = e(nc.sbuf_tensor("ztile", [128, 1026], F32))
        ps_pos = e(nc.psum_tensor("ps_pos", [128, NE], F32))
        ps_cc = e(nc.psum_tensor("ps_cc", [NE, 1], F32))

        p.dma("sync", eidx[:], i_eidx[:], writes=("eidx",))
        p.dma("sync", wn[:], i_wn[:], writes=("wn",))
        p.dma("sync", ct_t[:], i_ct[:], writes=("ct",))
        p.dma("sync", cm[:], i_cm[:], writes=("cm",))
        p.dma("sync", tok[:], i_tok[:], writes=("tok",))
        p.dma("sync", tris[:], i_tris[:], writes=("tris",))
        p.dma("sync", iota64[:], i_iota64[:], writes=("iota",))
        p.dma("sync", eb64[:], i_eb[:], writes=("eb",))
        p.dma("sync", db64[:], i_db[:], writes=("db",))
        p.step("vector", lambda E: E.memset(ones_bf[:], 1.0), writes=("ones_bf",))
        p.step("vector", lambda E: E.memset(ztile[:], 0.0), writes=("zt",))
        p.dma("gpsimd", bass.AP(o_part.tensor, 0, [[1026, 128], [1, 1026]]), ztile[:],
              reads=("zt",), writes=("part",))
        p.step("vector", lambda E: E.tensor_mul(cm[:], cm[:], ct_t[:]),
               reads=("cm", "ct"), writes=("cm",))
        p.step("vector", lambda E: E.reduce_sum(basecol[:], cm[:], axis=AX.X),
               reads=("cm",), writes=("basecol",))
        for t in range(16):
            p.dma("gpsimd", brow_d[0:1, :], basecol[:],
                  reads=("basecol",), writes=("brow",))
            p.dma("sync", base_b[:], bass.AP(brow_d.tensor, 0, [[0, 128], [1, NE]]),
                  reads=("brow",), writes=("base_b",))
            for k in range(6):
                p.step("vector", lambda E, t=t, k=k: E.tensor_scalar(
                    ohk[k][:], iota64[:], eidx[:, t * 6 + k:t * 6 + k + 1], None, op0=ALU.is_equal),
                       reads=("iota", "eidx"), writes=(f"ohk{k}",))
            p.step("vector", lambda E: E.tensor_copy(m6[:], ohk[0][:]),
                   reads=("ohk0",), writes=("m6",))
            for k in range(1, 6):
                p.step("vector", lambda E, k=k: E.tensor_add(m6[:], m6[:], ohk[k][:]),
                       reads=("m6", f"ohk{k}"), writes=("m6",))
            p.step("tensor", lambda E: E.matmul(ps_pos[:], tris[:], m6[:], start=True, stop=True),
                   reads=("tris", "m6"), writes=("ps_pos",))
            p.step("vector", lambda E: E.tensor_add(posT[:], ps_pos[:], base_b[:]),
                   reads=("ps_pos", "base_b"), writes=("posT",))
            p.step("tensor", lambda E: E.matmul(ps_cc[:], m6[:], ones_bf[:], start=True, stop=True),
                   reads=("m6", "ones_bf"), writes=("ps_cc",))
            p.step("vector", lambda E: E.tensor_add(basecol[:], basecol[:], ps_cc[:]),
                   reads=("basecol", "ps_cc"), writes=("basecol",))
            for k in range(6):
                kk = t * 6 + k
                p.step("vector", lambda E, k=k: E.tensor_mul(tmp64[:], posT[:], ohk[k][:]),
                       reads=("posT", f"ohk{k}"), writes=("tmp64",))
                p.step("vector", lambda E: E.reduce_sum(pcol[:], tmp64[:], axis=AX.X),
                       reads=("tmp64",), writes=("pcol",))
                p.step("vector", lambda E, k=k: E.tensor_mul(tmp64[:], eb64[:], ohk[k][:]),
                       reads=("eb", f"ohk{k}"), writes=("tmp64",))
                p.step("vector", lambda E: E.reduce_sum(ecol[:], tmp64[:], axis=AX.X),
                       reads=("tmp64",), writes=("ecol",))
                p.step("vector", lambda E, k=k: E.tensor_mul(tmp64[:], db64[:], ohk[k][:]),
                       reads=("db", f"ohk{k}"), writes=("tmp64",))
                p.step("vector", lambda E: E.reduce_sum(rd[:], tmp64[:], axis=AX.X),
                       reads=("tmp64",), writes=("rd",))
                p.step("vector", lambda E: E.tensor_add(r1[:], ecol[:], pcol[:]),
                       reads=("ecol", "pcol"), writes=("r1",))
                p.step("vector", lambda E: E.tensor_single_scalar(mcol[:], pcol[:], 1024.0, op=ALU.is_lt),
                       reads=("pcol",), writes=("mcol",))
                p.step("vector", lambda E: E.tensor_sub(rowf[:], r1[:], rd[:]),
                       reads=("r1", "rd"), writes=("rowf",))
                p.step("vector", lambda E: E.tensor_mul(rowf[:], rowf[:], mcol[:]),
                       reads=("rowf", "mcol"), writes=("rowf",))
                p.step("vector", lambda E: E.tensor_add(rowf[:], rowf[:], rd[:]),
                       reads=("rowf", "rd"), writes=("rowf",))
                p.step("vector", lambda E: E.tensor_copy(rowi[:], rowf[:]),
                       reads=("rowf",), writes=("rowi",))
                p.step("vector", lambda E, t=t: E.tensor_copy(content[:, 0:1], tok[:, t:t + 1]),
                       reads=("tok",), writes=("content",))
                p.step("vector", lambda E, kk=kk: E.tensor_copy(content[:, 1:2], wn[:, kk:kk + 1]),
                       reads=("wn",), writes=("content",))
                from concourse.bass_core_rust import IndirectOffsetOnAxis
                def scat(E):
                    return E.indirect_dma_start(
                        out=o_part, out_offset=IndirectOffsetOnAxis(ap=rowi[:], axis=0),
                        in_=content[:], in_offset=None)
                p.step("gpsimd", scat, reads=("rowi", "content", "part"),
                       writes=("part",), dma=True)
        p.build(st)
    return nc


def stage_D(inputs, core, moeC):
    ct_all = np.stack([moeC["ctot"][c][0] for c in range(8)], axis=1)
    cmask = np.zeros((NE, 8), np.float32)
    cmask[:, :core] = 1.0
    tokid = (core * TPC + np.arange(16)[None, :] * 128 +
             np.arange(128)[:, None]).astype(np.float32)
    tris = np.triu(np.ones((128, 128), np.float32), 1)
    iota = np.broadcast_to(np.arange(NE, dtype=np.float32), (128, NE))
    ee = np.arange(NE)
    eb = ((ee // 8) * EBLK + (ee % 8) * CAP).astype(np.float32)
    db = ((ee // 8) * EBLK + 8 * CAP).astype(np.float32)
    import ml_dtypes
    return {"e_idx": moeC["e_idx"][core], "wnorm": moeC["wnorm"][core],
            "ctot_t": np.ascontiguousarray(ct_all), "cmask": cmask,
            "tokid": tokid, "tris": tris.astype(ml_dtypes.bfloat16),
            "iota64": np.ascontiguousarray(iota),
            "ebase64": np.ascontiguousarray(np.broadcast_to(eb, (128, NE))),
            "dump64": np.ascontiguousarray(np.broadcast_to(db, (128, NE)))}


from concourse.bass_core_rust import IndirectOffsetOnAxis

def build_E():
    nc = make_nc()
    i_xtm = nc.dram_tensor("xtm", [8 * TPC, DM], BF16, kind="ExternalInput").ap()
    i_p8 = nc.dram_tensor("p8", [8 * EBLK, 2], F32, kind="ExternalInput").ap()
    i_wup = nc.dram_tensor("wup8", [8 * DM, 512], F32, kind="ExternalInput").ap()
    i_wdn = nc.dram_tensor("wdn8", [8 * 256, DM], F32, kind="ExternalInput").ap()
    o_pout = nc.dram_tensor("pout", [8 * TPC, DM], F32, kind="ExternalOutput").ap()
    tab_d = nc.dram_tensor("tab_d", [8 * CAP, 2], F32, kind="Internal").ap()
    xd_d = nc.dram_tensor("xd_d", [CAP, DM], BF16, kind="Internal").ap()

    p = P(nc)
    with ExitStack() as st:
        e = st.enter_context
        mrg = e(nc.sbuf_tensor("mrg", [128, 128], F32))
        tmp = e(nc.sbuf_tensor("tmp", [128, 128], F32))
        sinfo = e(nc.sbuf_tensor("sinfo", [128, 2], F32))
        itab = e(nc.sbuf_tensor("itab", [128, 64], I32))
        wtab = e(nc.sbuf_tensor("wtab", [128, 64], F32))
        xd_fm = e(nc.sbuf_tensor("xd_fm", [128, 16 * CAP], BF16))
        wup = [e(nc.sbuf_tensor(f"wup{i}", [128, 16 * 512], BF16)) for i in range(2)]
        wdn = [e(nc.sbuf_tensor(f"wdn{i}", [128, 2 * DM], BF16)) for i in range(2)]
        gtile = e(nc.sbuf_tensor("gtile", [128, DM], BF16))
        gsil = e(nc.sbuf_tensor("gsil", [128, CAP], F32))
        h_sb = e(nc.sbuf_tensor("h_sb", [128, 2 * CAP], BF16))
        orow = e(nc.sbuf_tensor("orow", [128, DM], F32))
        ztile = e(nc.sbuf_tensor("ztile2", [128, DM], F32))
        ps_all = e(nc.psum_tensor("ps_all", [128, 4096], F32))

        p.step("vector", lambda E: E.memset(ztile[:], 0.0), writes=("zt",))
        for b in range(16):
            p.dma("sync",
                  bass.AP(o_pout.tensor, b * 1024 * DM, [[DM, 128], [128 * DM, 8], [1, DM]]),
                  ztile[:].unsqueeze(1).broadcast_to([128, 8, DM]),
                  reads=("zt",), writes=("pout",))
        p.dma("sync", mrg[:], bass.AP(i_p8.tensor, 0, [[128, 128], [1, 128]]),
              writes=("mrg",))
        for q in range(1, 8):
            p.dma("sync", tmp[:], bass.AP(i_p8.tensor, q * EBLK * 2, [[128, 128], [1, 128]]),
                  writes=("tmpm",))
            p.step("vector", lambda E: E.tensor_add(mrg[:], mrg[:], tmp[:]),
                   reads=("mrg", "tmpm"), writes=("mrg",))
        p.dma("gpsimd", bass.AP(tab_d.tensor, 0, [[128, 128], [1, 128]]), mrg[:],
              reads=("mrg",), writes=("tab",))

        for ex in range(8):
            wb = ex % 2
            p.dma("gpsimd", wup[wb][:].rearrange("p (a j) -> p a j", a=16),
                  bass.AP(i_wup.tensor, ex * DM * 512, [[512, 128], [128 * 512, 16], [1, 512]]),
                  writes=(f"wup{wb}",))
            p.dma("gpsimd", wdn[wb][:].rearrange("p (a j) -> p a j", a=2),
                  bass.AP(i_wdn.tensor, ex * 256 * DM, [[DM, 128], [128 * DM, 2], [1, DM]]),
                  writes=(f"wdn{wb}",))
            for g in range(8):
                col = ex * 8 + g
                p.dma("sync", sinfo[:], tab_d[(ex * 8 + g) * 128:(ex * 8 + g + 1) * 128, :],
                      reads=("tab",), writes=("sinfo",))
                p.step("vector", lambda E, col=col: E.tensor_copy(itab[:, col:col + 1], sinfo[:, 0:1]),
                       reads=("sinfo",), writes=("itab",))
                p.step("vector", lambda E, col=col: E.tensor_copy(wtab[:, col:col + 1], sinfo[:, 1:2]),
                       reads=("sinfo",), writes=("wtab",))
                def gat(E, col=col):
                    return E.indirect_dma_start(
                        out=gtile[:], out_offset=None,
                        in_=i_xtm, in_offset=IndirectOffsetOnAxis(ap=itab[:, col:col + 1], axis=0))
                p.step("gpsimd", gat, reads=("itab",), writes=("gtile",), dma=True)
                p.dma("gpsimd", xd_d[g * 128:(g + 1) * 128, :], gtile[:],
                      reads=("gtile",), writes=("xd",))
            for dc in range(16):
                p.dma("sync", xd_fm[:, dc * CAP:(dc + 1) * CAP],
                      bass.AP(xd_d.tensor, dc * 128, [[DM, CAP], [1, 128]]),
                      reads=("xd",), writes=("xdfm",), transpose=True)
            for fc in range(4):
                def mmup(E, fc=fc, wb=wb):
                    last = None
                    for dc in range(16):
                        for o in (0, 512):
                            last = E.matmul(ps_all[:, fc * CAP + o: fc * CAP + o + 512],
                                            wup[wb][:, dc * 512 + fc * 128: dc * 512 + (fc + 1) * 128],
                                            xd_fm[:, dc * CAP + o: dc * CAP + o + 512],
                                            start=(dc == 0), stop=(dc == 15))
                    return last
                p.step("tensor", mmup, reads=(f"wup{wb}", "xdfm"), writes=("ps_all",))
            for hf in range(2):
                p.step("scalar", lambda E, hf=hf: E.activation(
                    out=gsil[:], in_=ps_all[:, hf * CAP:(hf + 1) * CAP], func=AF.Silu),
                       reads=("ps_all",), writes=("gsil",))
                p.step("vector", lambda E, hf=hf: E.tensor_mul(
                    h_sb[:, hf * CAP:(hf + 1) * CAP], gsil[:],
                    ps_all[:, (hf + 2) * CAP:(hf + 3) * CAP]),
                       reads=("gsil", "ps_all"), writes=("h_sb",))
            for sg in range(8):
                def mmdn(E, sg=sg, wb=wb):
                    last = None
                    for hf in range(2):
                        for dmb in range(4):
                            last = E.matmul(ps_all[:, 2048 + dmb * 512: 2048 + (dmb + 1) * 512],
                                            h_sb[:, hf * CAP + sg * 128: hf * CAP + (sg + 1) * 128],
                                            wdn[wb][:, hf * DM + dmb * 512: hf * DM + (dmb + 1) * 512],
                                            start=(hf == 0), stop=(hf == 1))
                    return last
                p.step("tensor", mmdn, reads=("h_sb", f"wdn{wb}"), writes=("ps_all",))
                col = ex * 8 + sg
                p.step("vector", lambda E, col=col: E.tensor_scalar(
                    orow[:], ps_all[:, 2048:4096], wtab[:, col:col + 1], None, op0=ALU.mult),
                       reads=("ps_all", "wtab"), writes=("orow",))
                def scat(E, col=col):
                    return E.indirect_dma_start(
                        out=o_pout, out_offset=IndirectOffsetOnAxis(ap=itab[:, col:col + 1], axis=0),
                        in_=orow[:], in_offset=None, compute_op=ALU.add)
                p.step("gpsimd", scat, reads=("orow", "itab", "pout"), writes=("pout",), dma=True)
        p.build(st)
    return nc


def stage_E(inputs, core, moeC, moeD):
    import ml_dtypes
    xtm = np.concatenate([moeC["x2n_tm"][c] for c in range(8)], 0)
    p8 = np.concatenate([moeD["partial"][q][core * EBLK:(core + 1) * EBLK] for q in range(8)], 0)
    wup = inputs["L0_ffn_w_up"][core * 8:(core + 1) * 8].reshape(8 * DM, 512)
    wdn = inputs["L0_ffn_w_down"][core * 8:(core + 1) * 8].reshape(8 * 256, DM)
    return {"xtm": xtm, "p8": np.ascontiguousarray(p8),
            "wup8": np.ascontiguousarray(wup), "wdn8": np.ascontiguousarray(wdn)}


def build_F():
    nc = make_nc()
    i_x1 = nc.dram_tensor("x1", [DM, TPC], F32, kind="ExternalInput").ap()
    i_sh = nc.dram_tensor("shared", [DM, TPC], F32, kind="ExternalInput").ap()
    i_pf = nc.dram_tensor("pf", [8 * DM, TPC], F32, kind="ExternalInput").ap()
    o_x2 = nc.dram_tensor("x2", [DM, TPC], F32, kind="ExternalOutput").ap()
    p = P(nc)
    with ExitStack() as st:
        e = st.enter_context
        acc = e(nc.sbuf_tensor("acc", [128, TPC], F32))
        tin = [e(nc.sbuf_tensor(f"tin{i}", [128, TPC], F32)) for i in range(2)]
        for dc in range(16):
            p.dma("sync", acc[:], i_x1[dc * 128:(dc + 1) * 128, :], writes=("acc",))
            p.dma("sync", tin[0][:], i_sh[dc * 128:(dc + 1) * 128, :], writes=("t0",))
            p.step("vector", lambda E: E.tensor_add(acc[:], acc[:], tin[0][:]),
                   reads=("acc", "t0"), writes=("acc",))
            for q in range(8):
                b = q % 2
                p.dma("sync", tin[b][:], i_pf[q * DM + dc * 128: q * DM + (dc + 1) * 128, :],
                      writes=(f"t{b}",))
                p.step("vector", lambda E, b=b: E.tensor_add(acc[:], acc[:], tin[b][:]),
                       reads=("acc", f"t{b}"), writes=("acc",))
            p.dma("gpsimd", o_x2[dc * 128:(dc + 1) * 128, :], acc[:],
                  reads=("acc",), writes=("o",))
        p.build(st)
    return nc


def stage_F(inputs, core, x1, moeC, pout):
    pf = np.concatenate([np.ascontiguousarray(pout[q][core * TPC:(core + 1) * TPC].T)
                         for q in range(8)], 0)
    return {"x1": x1[core], "shared": moeC["shared"][core], "pf": pf}


NCH = TPC // 64
NH = 16

def build_G():
    nc = make_nc()
    i_x2 = nc.dram_tensor("x2", [DM, TPC], F32, kind="ExternalInput").ap()
    i_wqkv = nc.dram_tensor("wqkv", [DM, 3 * DM], F32, kind="ExternalInput").ap()
    i_bqkv = nc.dram_tensor("bqkv", [128, 48], F32, kind="ExternalInput").ap()
    i_sl = nc.dram_tensor("slopes", [1, NH], F32, kind="ExternalInput").ap()
    i_dT = nc.dram_tensor("diffT", [64, 64], F32, kind="ExternalInput").ap()
    i_mT = nc.dram_tensor("maskT", [64, 64], F32, kind="ExternalInput").ap()
    i_irow = nc.dram_tensor("iota_r", [128, 64], F32, kind="ExternalInput").ap()
    i_rev = nc.dram_tensor("rev128", [128, 1], F32, kind="ExternalInput").ap()
    i_idbf = nc.dram_tensor("ident_bf", [128, 128], BF16, kind="ExternalInput").ap()
    o_attn = nc.dram_tensor("attn_tm", [TPC, DM], F32, kind="ExternalOutput").ap()
    o_qfm = nc.dram_tensor("q_fm", [DM, TPC], BF16, kind="ExternalOutput").ap()
    o_send = nc.dram_tensor("s_end", [NH * 128, 128], F32, kind="ExternalOutput").ap()
    rows_d = nc.dram_tensor("rows_d", [2, TPC], F32, kind="Internal").ap()

    p = P(nc)
    with ExitStack() as st:
        e = st.enter_context
        xbuf = [e(nc.sbuf_tensor(f"xbuf{i}", [128, TPC], F32)) for i in range(2)]
        sqt = e(nc.sbuf_tensor("sqt", [128, TPC], F32))
        ones1 = e(nc.sbuf_tensor("ones1", [128, 1], F32))
        consts = e(nc.sbuf_tensor("consts", [128, 4], F32))
        xn_sb = e(nc.sbuf_tensor("xn_sb", [128, 16 * TPC], BF16))
        mu_b = e(nc.sbuf_tensor("mu_b", [128, TPC], F32))
        rstd_b = e(nc.sbuf_tensor("rstd_b", [128, TPC], F32))
        idbf = e(nc.sbuf_tensor("idbf", [128, 128], BF16))
        wt = [e(nc.sbuf_tensor(f"wt{i}", [128, 2048], BF16)) for i in range(2)]
        bq_sb = e(nc.sbuf_tensor("bq_sb", [128, 48], F32))
        slb = e(nc.sbuf_tensor("slb", [128, NH], F32))
        nslb = e(nc.sbuf_tensor("nslb", [128, NH], F32))
        bd_sb = e(nc.sbuf_tensor("bd_sb", [128, NH], F32))
        dT = e(nc.sbuf_tensor("dT", [64, 64], F32))
        mT = e(nc.sbuf_tensor("mT", [64, 64], F32))
        irow = e(nc.sbuf_tensor("irow", [128, 64], F32))
        rev = e(nc.sbuf_tensor("rev", [128, 1], F32))
        dect = e(nc.sbuf_tensor("dect", [64, 64], F32))
        qd_b = e(nc.sbuf_tensor("qd_b", [128, 64], F32))
        kd64 = e(nc.sbuf_tensor("kd64", [64, 1], F32))
        qfm = e(nc.sbuf_tensor("qfm", [128, TPC], BF16))
        kfm = e(nc.sbuf_tensor("kfm", [128, TPC], BF16))
        vfm = e(nc.sbuf_tensor("vfm", [128, TPC], BF16))
        ktm = e(nc.sbuf_tensor("ktm", [64, NCH * 128], BF16))
        vtm = e(nc.sbuf_tensor("vtm", [64, NCH * 128], BF16))
        qbp = e(nc.sbuf_tensor("qbp", [128, 64], BF16))
        sc_sb = e(nc.sbuf_tensor("sc_sb", [64, 64], BF16))
        S_sb = e(nc.sbuf_tensor("S_sb", [128, 128], F32))
        S_bf = e(nc.sbuf_tensor("S_bf", [128, 128], BF16))
        ah = e(nc.sbuf_tensor("ah", [64, NCH * 128], F32))
        ps_a = e(nc.psum_tensor("ps_a", [128, 2048], F32))
        ps_t = e(nc.psum_tensor("ps_t", [128, 128], BF16))
        ps_sc = e(nc.psum_tensor("ps_sc", [64, 64], F32))
        ps_o = e(nc.psum_tensor("ps_o", [64, 128], F32))
        ps_S = e(nc.psum_tensor("ps_S", [128, 128], F32))

        p.step("vector", lambda E: E.memset(ones1[:], 1.0), writes=("ones1",))
        p.step("vector", lambda E: E.memset(consts[:, 0:1], EPS), writes=("consts",))
        for t, src, key in ((idbf, i_idbf, "idbf"), (bq_sb, i_bqkv, "bq"), (dT, i_dT, "dT"),
                            (mT, i_mT, "mT"), (irow, i_irow, "irow"), (rev, i_rev, "rev")):
            p.dma("sync", t[:], src[:], writes=(key,))
        p.dma("sync", slb[:], i_sl.partition_broadcast(128), writes=("slb",))
        p.step("vector", lambda E: E.tensor_scalar_mul(nslb[:], slb[:], -1.0),
               reads=("slb",), writes=("nslb",))
        p.step("vector", lambda E: E.tensor_scalar_mul(bd_sb[:], slb[:], -64.0),
               reads=("slb",), writes=("bd",))
        p.step("scalar", lambda E: E.activation(out=bd_sb[:], in_=bd_sb[:], func=AF.Exp),
               reads=("bd",), writes=("bd",))

        for dc in range(16):
            b = dc % 2
            p.dma("sync", xbuf[b][:], i_x2[dc * 128:(dc + 1) * 128, :], writes=(f"xbuf{b}",))
            def mmrow(E, dc=dc, b=b):
                last = None
                for (o, n) in NT_FULL:
                    last = E.matmul(ps_a[0:1, o:o + n], ones1[:], xbuf[b][:, o:o + n],
                                    start=(dc == 0), stop=(dc == 15))
                return last
            p.step("tensor", mmrow, reads=(f"xbuf{b}", "ones1"), writes=("ps_a",))
        p.step("vector", lambda E: E.tensor_scalar_mul(mu_b[0:1, :], ps_a[0:1, 0:TPC], 1.0 / DM),
               reads=("ps_a",), writes=("murow",))
        for dc in range(16):
            b = dc % 2
            p.dma("sync", xbuf[b][:], i_x2[dc * 128:(dc + 1) * 128, :], writes=(f"xbuf{b}",))
            p.step("scalar", lambda E, b=b: E.activation(out=sqt[:], in_=xbuf[b][:], func=AF.Square),
                   reads=(f"xbuf{b}",), writes=("sqt",))
            def mmrow2(E, dc=dc, b=b):
                last = None
                for (o, n) in NT_FULL:
                    last = E.matmul(ps_a[0:1, o:o + n], ones1[:], sqt[:, o:o + n],
                                    start=(dc == 0), stop=(dc == 15))
                return last
            p.step("tensor", mmrow2, reads=("sqt", "ones1"), writes=("ps_a",))
        p.step("vector", lambda E: E.tensor_scalar_mul(rstd_b[0:1, :], ps_a[0:1, 0:TPC], 1.0 / DM),
               reads=("ps_a",), writes=("varow",))
        p.step("vector", lambda E: E.tensor_mul(sqt[0:1, 0:TPC], mu_b[0:1, :], mu_b[0:1, :]),
               reads=("murow",), writes=("sqrow",))
        p.step("vector", lambda E: E.tensor_sub(rstd_b[0:1, :], rstd_b[0:1, :], sqt[0:1, 0:TPC]),
               reads=("sqrow", "varow"), writes=("varow",))
        p.step("scalar", lambda E: E.activation(out=rstd_b[0:1, :], in_=rstd_b[0:1, :],
                                                func=AF.Sqrt, bias=consts[0:1, 0:1]),
               reads=("varow", "consts"), writes=("varow",))
        p.step("vector", lambda E: E.reciprocal(rstd_b[0:1, :], rstd_b[0:1, :]),
               reads=("varow",), writes=("varow",))
        p.dma("gpsimd", rows_d[0:1, :], mu_b[0:1, :], reads=("murow",), writes=("rows_d",))
        p.dma("gpsimd", rows_d[1:2, :], rstd_b[0:1, :], reads=("varow",), writes=("rows_d",))
        p.dma("sync", mu_b[:], rows_d[0:1, :].partition_broadcast(128),
              reads=("rows_d",), writes=("mu_b",))
        p.dma("sync", rstd_b[:], rows_d[1:2, :].partition_broadcast(128),
              reads=("rows_d",), writes=("rstd_b",))
        for dc in range(16):
            b = dc % 2
            p.dma("sync", xbuf[b][:], i_x2[dc * 128:(dc + 1) * 128, :], writes=(f"xbuf{b}",))
            p.step("vector", lambda E, b=b: E.tensor_sub(sqt[:], xbuf[b][:], mu_b[:]),
                   reads=(f"xbuf{b}", "mu_b"), writes=("sqt",))
            p.step("vector", lambda E, dc=dc: E.tensor_mul(
                xn_sb[:, dc * TPC:(dc + 1) * TPC], sqt[:], rstd_b[:]),
                   reads=("sqt", "rstd_b"), writes=("xn",))

        for h in range(NH):
            p.step("scalar", lambda E, h=h: E.activation(out=dect[:], in_=dT[:], func=AF.Exp,
                                                         scale=nslb[0:64, h:h + 1]),
                   reads=("dT", "nslb"), writes=("dect",))
            p.step("vector", lambda E: E.tensor_mul(dect[:], dect[:], mT[:]),
                   reads=("dect", "mT"), writes=("dect",))
            p.step("scalar", lambda E, h=h: E.activation(out=qd_b[:], in_=irow[:], func=AF.Exp,
                                                         scale=nslb[:, h:h + 1]),
                   reads=("irow", "nslb"), writes=("qd_b",))
            p.step("scalar", lambda E, h=h: E.activation(out=kd64[:], in_=rev[0:64, :], func=AF.Exp,
                                                         scale=nslb[0:64, h:h + 1]),
                   reads=("rev", "nslb"), writes=("kd64",))
            for which, dst, fc0, key in (("q", qfm, h, "qfm"), ("k", kfm, 16 + h, "kfm"),
                                         ("v", vfm, 32 + h, "vfm")):
                b = h % 2
                src = bass.AP(i_wqkv.tensor, fc0 * 128, [[3 * DM, 128], [128 * 3 * DM, 16], [1, 128]])
                p.dma("gpsimd", wt[b][:].rearrange("p (a j) -> p a j", a=16), src,
                      writes=(f"wt{b}",))
                def mmq(E, b=b):
                    last = None
                    for dc in range(16):
                        for (o, n) in NT_FULL:
                            last = E.matmul(ps_a[:, o:o + n], wt[b][:, dc * 128:(dc + 1) * 128],
                                            xn_sb[:, dc * TPC + o: dc * TPC + o + n],
                                            start=(dc == 0), stop=(dc == 15))
                    return last
                p.step("tensor", mmq, reads=(f"wt{b}", "xn"), writes=("ps_a",))
                p.step("vector", lambda E, dst=dst, fc0=fc0: E.tensor_scalar(
                    dst[:], ps_a[:, 0:TPC], bq_sb[:, fc0:fc0 + 1], None, op0=ALU.add),
                       reads=("ps_a", "bq"), writes=(key,))
            for n in range(NCH):
                p.step("tensor", lambda E, n=n: E.transpose(
                    ps_t[0:64, :], kfm[:, n * 64:(n + 1) * 64], idbf[:]),
                       reads=("kfm", "idbf"), writes=("ps_t",))
                p.step("vector", lambda E, n=n: E.tensor_copy(ktm[:, n * 128:(n + 1) * 128], ps_t[0:64, :]),
                       reads=("ps_t",), writes=("ktm",))
                p.step("tensor", lambda E, n=n: E.transpose(
                    ps_t[0:64, :], vfm[:, n * 64:(n + 1) * 64], idbf[:]),
                       reads=("vfm", "idbf"), writes=("ps_t",))
                p.step("vector", lambda E, n=n: E.tensor_copy(vtm[:, n * 128:(n + 1) * 128], ps_t[0:64, :]),
                       reads=("ps_t",), writes=("vtm",))
            p.step("vector", lambda E: E.tensor_scalar_mul(ktm[:], ktm[:], kd64[:]),
                   reads=("ktm", "kd64"), writes=("ktm",))
            p.step("vector", lambda E: E.memset(S_sb[:], 0.0), writes=("S_sb",))
            for n in range(NCH):
                sl = slice(n * 64, (n + 1) * 64)
                csl = slice(n * 128, (n + 1) * 128)
                p.step("tensor", lambda E, sl=sl: E.matmul(
                    ps_sc[:], kfm[:, sl], qfm[:, sl], start=True, stop=True),
                       reads=("kfm", "qfm"), writes=("ps_sc",))
                p.step("vector", lambda E: E.tensor_mul(sc_sb[:], ps_sc[:], dect[:]),
                       reads=("ps_sc", "dect"), writes=("sc_sb",))
                p.step("vector", lambda E, sl=sl: E.tensor_mul(
                    qbp[:], qfm[:, sl], qd_b[:]),
                       reads=("qfm", "qd_b"), writes=("qbp",))
                p.step("vector", lambda E: E.tensor_copy(S_bf[:], S_sb[:]),
                       reads=("S_sb",), writes=("S_bf",))
                def mmo(E, csl=csl):
                    E.matmul(ps_o[:], sc_sb[:], vtm[:, csl], start=True, stop=False)
                    return E.matmul(ps_o[:], qbp[:], S_bf[:], start=False, stop=True)
                p.step("tensor", mmo, reads=("sc_sb", "vtm", "qbp", "S_bf"), writes=("ps_o",))
                p.step("vector", lambda E, csl=csl: E.tensor_copy(ah[:, csl], ps_o[:]),
                       reads=("ps_o",), writes=("ah",))
                def mmS(E, csl=csl):
                    return E.matmul(ps_S[:], ktm[:, csl], vtm[:, csl], start=True, stop=True)
                p.step("tensor", mmS, reads=("ktm", "vtm"), writes=("ps_S",))
                p.step("vector", lambda E, h=h: E.scalar_tensor_tensor(
                    S_sb[:], S_sb[:], bd_sb[:, h:h + 1], ps_S[:], op0=ALU.mult, op1=ALU.add),
                       reads=("S_sb", "bd", "ps_S"), writes=("S_sb",))
            p.dma("gpsimd",
                  bass.AP(o_attn.tensor, h * 128, [[DM, 64], [64 * DM, NCH], [1, 128]]),
                  ah[:].rearrange("p (a j) -> p a j", a=NCH),
                  reads=("ah",), writes=("o_attn",))
            p.dma("gpsimd", o_qfm[h * 128:(h + 1) * 128, :], qfm[:],
                  reads=("qfm",), writes=("o_qfm",))
            p.dma("gpsimd", o_send[h * 128:(h + 1) * 128, :], S_sb[:],
                  reads=("S_sb",), writes=("o_send",))
        p.build(st)
    return nc


def stage_G(inputs, core, x2):
    i = np.arange(64, dtype=np.float32)
    diffT = i[None, :] - i[:, None]
    maskT = (diffT >= 0).astype(np.float32)
    irow = np.broadcast_to(i, (128, 64))
    rev = (64.0 - np.arange(128, dtype=np.float32) % 64).reshape(128, 1)
    ident = np.eye(128, dtype=np.float32)
    import ml_dtypes
    return {"x2": x2[core], "wqkv": inputs["L1_attn_Wqkv"],
            "bqkv": np.ascontiguousarray(inputs["L1_attn_bqkv"].reshape(48, 128).T),
            "slopes": inputs["L1_attn_slopes"].reshape(1, NH),
            "diffT": np.ascontiguousarray(diffT), "maskT": maskT,
            "iota_r": np.ascontiguousarray(irow), "rev128": rev,
            "ident_bf": ident.astype(ml_dtypes.bfloat16)}


def build_H():
    nc = make_nc()
    i_attn = nc.dram_tensor("attn_tm", [TPC, DM], F32, kind="ExternalInput").ap()
    i_qfm = nc.dram_tensor("q_fm", [DM, TPC], BF16, kind="ExternalInput").ap()
    i_send = nc.dram_tensor("send_all", [8 * NH * 128, 128], F32, kind="ExternalInput").ap()
    i_sl = nc.dram_tensor("slopes", [1, NH], F32, kind="ExternalInput").ap()
    i_mask = nc.dram_tensor("mask", [128, 8], F32, kind="ExternalInput").ap()
    i_irow = nc.dram_tensor("iota_r", [128, 64], F32, kind="ExternalInput").ap()
    i_nrow = nc.dram_tensor("nrow", [128, NCH], F32, kind="ExternalInput").ap()
    i_idbf = nc.dram_tensor("ident_bf", [128, 128], BF16, kind="ExternalInput").ap()
    i_wout = nc.dram_tensor("wout", [DM, DM], F32, kind="ExternalInput").ap()
    i_bout = nc.dram_tensor("bout", [128, 16], F32, kind="ExternalInput").ap()
    i_x2 = nc.dram_tensor("x2", [DM, TPC], F32, kind="ExternalInput").ap()
    o_x3 = nc.dram_tensor("x3", [DM, TPC], F32, kind="ExternalOutput").ap()
    afm_d = nc.dram_tensor("afm_d", [DM, TPC], BF16, kind="Internal").ap()

    p = P(nc)
    with ExitStack() as st:
        e = st.enter_context
        slb = e(nc.sbuf_tensor("slb", [128, NH], F32))
        nslb = e(nc.sbuf_tensor("nslb", [128, NH], F32))
        t64 = e(nc.sbuf_tensor("t64", [128, NH], F32))
        bd32 = e(nc.sbuf_tensor("bd32", [128, NH], F32))
        bdm1 = e(nc.sbuf_tensor("bdm1", [128, NH], F32))
        bdn = e(nc.sbuf_tensor("bdn", [128, NCH], F32))
        mask_sb = e(nc.sbuf_tensor("mask_sb", [128, 8], F32))
        irow = e(nc.sbuf_tensor("irow", [128, 64], F32))
        nrow = e(nc.sbuf_tensor("nrow_sb", [128, NCH], F32))
        idbf = e(nc.sbuf_tensor("idbf", [128, 128], BF16))
        qd_b = e(nc.sbuf_tensor("qd_b", [128, 64], F32))
        qfm = e(nc.sbuf_tensor("qfm", [128, TPC], BF16))
        qbp = e(nc.sbuf_tensor("qbp", [128, 64], BF16))
        send = e(nc.sbuf_tensor("send", [128, 128], F32))
        S_in = e(nc.sbuf_tensor("S_in", [128, 128], F32))
        S_bf = e(nc.sbuf_tensor("S_bf", [128, 128], BF16))
        fc_ = e(nc.sbuf_tensor("fc_", [128, 1], F32))
        t2 = e(nc.sbuf_tensor("t2", [128, 128], F32))
        ah = e(nc.sbuf_tensor("ah", [64, NCH * 128], F32))
        abf = e(nc.sbuf_tensor("abf", [64, NCH * 128], BF16))
        afm = e(nc.sbuf_tensor("afm", [128, NCH * 64], BF16))
        wo = [e(nc.sbuf_tensor(f"wo{i}", [128, DM], BF16)) for i in range(6)]
        ain = [e(nc.sbuf_tensor(f"ain{i}", [128, TB], BF16)) for i in range(2)]
        xin = e(nc.sbuf_tensor("xin", [128, TB], F32))
        x3t = e(nc.sbuf_tensor("x3t", [128, TB], F32))
        bo_sb = e(nc.sbuf_tensor("bo_sb", [128, 16], F32))
        ps_o = e(nc.psum_tensor("ps_o", [64, 128], F32))
        ps_t = e(nc.psum_tensor("ps_t", [128, 128], BF16))
        ps_y = [e(nc.psum_tensor(f"ps_y{i}", [128, TB], F32)) for i in range(6)]

        for t, src, key in ((mask_sb, i_mask, "mask"), (irow, i_irow, "irow"),
                            (nrow, i_nrow, "nrow"), (idbf, i_idbf, "idbf"),
                            (bo_sb, i_bout, "bo")):
            p.dma("sync", t[:], src[:], writes=(key,))
        p.dma("sync", slb[:], i_sl.partition_broadcast(128), writes=("slb",))
        p.step("vector", lambda E: E.tensor_scalar_mul(nslb[:], slb[:], -1.0),
               reads=("slb",), writes=("nslb",))
        p.step("vector", lambda E: E.tensor_scalar_mul(t64[:], slb[:], -64.0),
               reads=("slb",), writes=("t64",))
        p.step("vector", lambda E: E.tensor_scalar_mul(bd32[:], slb[:], -2048.0),
               reads=("slb",), writes=("bd32",))
        p.step("scalar", lambda E: E.activation(out=bd32[:], in_=bd32[:], func=AF.Exp),
               reads=("bd32",), writes=("bd32",))
        p.step("vector", lambda E: E.tensor_scalar_add(bdm1[:], bd32[:], -1.0),
               reads=("bd32",), writes=("bdm1",))

        for h in range(NH):
            p.step("scalar", lambda E, h=h: E.activation(out=bdn[:], in_=nrow[:], func=AF.Exp,
                                                         scale=t64[:, h:h + 1]),
                   reads=("nrow", "t64"), writes=("bdn",))
            p.step("scalar", lambda E, h=h: E.activation(out=qd_b[:], in_=irow[:], func=AF.Exp,
                                                         scale=nslb[:, h:h + 1]),
                   reads=("irow", "nslb"), writes=("qd_b",))
            p.step("vector", lambda E: E.memset(S_in[:], 0.0), writes=("S_in",))
            for q in range(8):
                p.dma("sync", send[:], i_send[(q * NH + h) * 128:(q * NH + h + 1) * 128, :],
                      writes=("send",))
                p.step("vector", lambda E, q=q, h=h: E.tensor_mul(
                    fc_[:], mask_sb[:, q:q + 1], bdm1[:, h:h + 1]),
                       reads=("mask", "bdm1"), writes=("fc_",))
                p.step("vector", lambda E: E.tensor_scalar_add(fc_[:], fc_[:], 1.0),
                       reads=("fc_",), writes=("fc_",))
                p.step("vector", lambda E, q=q: E.tensor_scalar_mul(
                    t2[:], send[:], mask_sb[:, q:q + 1]),
                       reads=("send", "mask"), writes=("t2",))
                p.step("vector", lambda E: E.scalar_tensor_tensor(
                    S_in[:], S_in[:], fc_[:], t2[:], op0=ALU.mult, op1=ALU.add),
                       reads=("S_in", "fc_", "t2"), writes=("S_in",))
            p.step("vector", lambda E: E.tensor_copy(S_bf[:], S_in[:]),
                   reads=("S_in",), writes=("S_bf",))
            p.dma("sync", qfm[:], i_qfm[h * 128:(h + 1) * 128, :], writes=("qfm",))
            p.dma("sync", ah[:].rearrange("p (a j) -> p a j", a=NCH),
                  bass.AP(i_attn.tensor, h * 128, [[DM, 64], [64 * DM, NCH], [1, 128]]),
                  writes=("ah",))
            for n in range(NCH):
                sl = slice(n * 64, (n + 1) * 64)
                csl = slice(n * 128, (n + 1) * 128)
                p.step("vector", lambda E, sl=sl: E.tensor_mul(qbp[:], qfm[:, sl], qd_b[:]),
                       reads=("qfm", "qd_b"), writes=("qbp",))
                p.step("tensor", lambda E: E.matmul(ps_o[:], qbp[:], S_bf[:], start=True, stop=True),
                       reads=("qbp", "S_bf"), writes=("ps_o",))
                p.step("vector", lambda E, n=n, csl=csl: E.scalar_tensor_tensor(
                    ah[:, csl], ps_o[:], bdn[0:64, n:n + 1],
                    ah[:, csl], op0=ALU.mult, op1=ALU.add),
                       reads=("ps_o", "bdn", "ah"), writes=("ah",))
            p.step("vector", lambda E: E.tensor_copy(abf[:], ah[:]),
                   reads=("ah",), writes=("abf",))
            for n in range(NCH):
                p.step("tensor", lambda E, n=n: E.transpose(
                    ps_t[:, 0:64], abf[:, n * 128:(n + 1) * 128], idbf[0:64, 0:64]),
                       reads=("abf", "idbf"), writes=("ps_t",))
                p.step("vector", lambda E, n=n: E.tensor_copy(afm[:, n * 64:(n + 1) * 64], ps_t[:, 0:64]),
                       reads=("ps_t",), writes=("afm",))
            p.dma("gpsimd", afm_d[h * 128:(h + 1) * 128, :], afm[:],
                  reads=("afm",), writes=("afm_d",))
        for grp, g0, gn in ((0, 0, 6), (1, 6, 6), (2, 12, 4)):
            for dm in range(gn):
                dmg = g0 + dm
                src = bass.AP(i_wout.tensor, dmg * 128, [[DM, 128], [128 * DM, 16], [1, 128]])
                p.dma("gpsimd", wo[dm][:].rearrange("p (a j) -> p a j", a=16), src,
                      writes=(f"wo{dm}",))
            for tb4 in range(NTB):
                t0 = tb4 * TB
                for c in range(16):
                    b = c % 2
                    p.dma("sync", ain[b][:], afm_d[c * 128:(c + 1) * 128, t0:t0 + TB],
                          reads=("afm_d",), writes=(f"ain{b}",))
                    def mmout(E, b=b, c=c, gn=gn):
                        last = None
                        for dm in range(gn):
                            last = E.matmul(ps_y[dm][:], wo[dm][:, c * 128:(c + 1) * 128],
                                            ain[b][:], start=(c == 0), stop=(c == 15))
                        return last
                    p.step("tensor", mmout,
                           reads=(f"ain{b}",) + tuple(f"wo{dm}" for dm in range(gn)),
                           writes=tuple(f"ps{dm}" for dm in range(gn)))
                for dm in range(gn):
                    dmg = g0 + dm
                    p.dma("sync", xin[:], i_x2[dmg * 128:(dmg + 1) * 128, t0:t0 + TB],
                          writes=("xin",))
                    p.step("vector", lambda E, dm=dm, dmg=dmg: E.tensor_scalar(
                        x3t[:], ps_y[dm][:], bo_sb[:, dmg:dmg + 1], None, op0=ALU.add),
                           reads=(f"ps{dm}", "bo"), writes=("x3t",))
                    p.step("vector", lambda E: E.tensor_add(x3t[:], x3t[:], xin[:]),
                           reads=("x3t", "xin"), writes=("x3t",))
                    p.dma("gpsimd", o_x3[dmg * 128:(dmg + 1) * 128, t0:t0 + TB], x3t[:],
                          reads=("x3t",), writes=("o_x3",))
        p.build(st)
    return nc


def stage_H(inputs, core, x2, attnG):
    i = np.arange(64, dtype=np.float32)
    irow = np.broadcast_to(i, (128, 64))
    nrow = np.broadcast_to(np.arange(NCH, dtype=np.float32), (128, NCH))
    ident = np.eye(128, dtype=np.float32)
    send_all = np.concatenate([attnG["s_end"][c] for c in range(8)], 0)
    mask = np.broadcast_to(chain_mask(core), (128, 8))
    import ml_dtypes
    return {"attn_tm": attnG["attn_tm"][core], "q_fm": attnG["q_fm"][core],
            "send_all": np.ascontiguousarray(send_all),
            "slopes": inputs["L1_attn_slopes"].reshape(1, NH),
            "mask": np.ascontiguousarray(mask),
            "iota_r": np.ascontiguousarray(irow), "nrow": np.ascontiguousarray(nrow),
            "ident_bf": ident.astype(ml_dtypes.bfloat16),
            "wout": inputs["L1_attn_Wout"],
            "bout": np.ascontiguousarray(inputs["L1_attn_bout"].reshape(16, 128).T),
            "x2": x2[core]}


def build_I1():
    nc = make_nc()
    i_w1 = nc.dram_tensor("w1s", [1024, DM], F32, kind="ExternalInput").ap()
    i_w2 = nc.dram_tensor("w2s", [256, 8192], F32, kind="ExternalInput").ap()
    o_part = nc.dram_tensor("parts", [1, 2], F32, kind="ExternalOutput").ap()
    p = P(nc)
    with ExitStack() as st:
        e = st.enter_context
        wt = e(nc.sbuf_tensor("wt", [128, 2048], F32))
        at = e(nc.sbuf_tensor("at", [128, 2048], F32))
        acc = e(nc.sbuf_tensor("acc2", [128, 2], F32))
        col = e(nc.sbuf_tensor("col", [128, 1], F32))
        ones1 = e(nc.sbuf_tensor("ones1", [128, 1], F32))
        orow = e(nc.sbuf_tensor("orow2", [1, 2], F32))
        ps = e(nc.psum_tensor("ps", [1, 2], F32))
        p.step("vector", lambda E: E.memset(ones1[:], 1.0), writes=("ones1",))
        p.step("vector", lambda E: E.memset(acc[:], 0.0), writes=("acc",))
        for which, src, nchunk, key in ((0, i_w1, 8, "w1"), (1, i_w2, 8, "w2")):
            for c in range(nchunk):
                if which == 0:
                    ap = src[c * 128:(c + 1) * 128, :]
                else:
                    ap = bass.AP(src.tensor, c * 128 * 2048, [[2048, 128], [1, 2048]])
                p.dma("sync", wt[:], ap, writes=("wt",))
                p.step("scalar", lambda E: E.activation(out=at[:], in_=wt[:], func=AF.Abs),
                       reads=("wt",), writes=("at",))
                p.step("vector", lambda E: E.reduce_sum(col[:], at[:], axis=AX.X),
                       reads=("at",), writes=("col",))
                p.step("vector", lambda E, which=which: E.tensor_add(
                    acc[:, which:which + 1], acc[:, which:which + 1], col[:]),
                       reads=("acc", "col"), writes=("acc",))
        p.step("tensor", lambda E: E.matmul(ps[:], ones1[:], acc[:], start=True, stop=True),
               reads=("ones1", "acc"), writes=("ps",))
        p.step("vector", lambda E: E.tensor_copy(orow[:], ps[:]),
               reads=("ps",), writes=("orow",))
        p.dma("gpsimd", o_part[:], orow[:], reads=("orow",), writes=("o",))
        p.build(st)
    return nc


def build_I2():
    nc = make_nc()
    i_w1 = nc.dram_tensor("w1s", [1024, DM], F32, kind="ExternalInput").ap()
    i_w2 = nc.dram_tensor("w2s", [256, 8192], F32, kind="ExternalInput").ap()
    i_parts = nc.dram_tensor("parts", [1, 16], F32, kind="ExternalInput").ap()
    o_w1q = nc.dram_tensor("w1q", [1024, DM], BF16, kind="ExternalOutput").ap()
    o_w2q = nc.dram_tensor("w2q", [256, 8192], BF16, kind="ExternalOutput").ap()
    thr_d = nc.dram_tensor("thr_d", [1, 2], F32, kind="Internal").ap()
    p = P(nc)
    with ExitStack() as st:
        e = st.enter_context
        wt = e(nc.sbuf_tensor("wt", [128, 2048], F32))
        at = e(nc.sbuf_tensor("at", [128, 2048], F32))
        st_ = e(nc.sbuf_tensor("st_", [128, 2048], F32))
        mt = e(nc.sbuf_tensor("mt", [128, 2048], F32))
        qt = e(nc.sbuf_tensor("qt", [128, 2048], BF16))
        pr = e(nc.sbuf_tensor("pr", [1, 16], F32))
        thr = e(nc.sbuf_tensor("thr", [1, 2], F32))
        thr_b = e(nc.sbuf_tensor("thr_b", [128, 2], F32))
        p.dma("sync", pr[:], i_parts[:], writes=("pr",))
        p.step("vector", lambda E: E.reduce_sum(thr[:, 0:1], pr[:, 0:8], axis=AX.X),
               reads=("pr",), writes=("thr",))
        p.step("vector", lambda E: E.reduce_sum(thr[:, 1:2], pr[:, 8:16], axis=AX.X),
               reads=("pr",), writes=("thr",))
        p.step("vector", lambda E: E.tensor_scalar_mul(thr[:], thr[:], 0.7 / (8192 * 2048)),
               reads=("thr",), writes=("thr",))
        p.dma("gpsimd", thr_d[:], thr[:], reads=("thr",), writes=("thr_d",))
        p.dma("sync", thr_b[:], thr_d.partition_broadcast(128), reads=("thr_d",), writes=("thr_b",))
        for which, src, dst, nchunk in ((0, i_w1, o_w1q, 8), (1, i_w2, o_w2q, 8)):
            for c in range(nchunk):
                if which == 0:
                    ap = src[c * 128:(c + 1) * 128, :]
                    dap = dst[c * 128:(c + 1) * 128, :]
                else:
                    ap = bass.AP(src.tensor, c * 128 * 2048, [[2048, 128], [1, 2048]])
                    dap = bass.AP(dst.tensor, c * 128 * 2048, [[2048, 128], [1, 2048]])
                p.dma("sync", wt[:], ap, writes=("wt",))
                p.step("scalar", lambda E: E.activation(out=at[:], in_=wt[:], func=AF.Abs),
                       reads=("wt",), writes=("at",))
                p.step("scalar", lambda E: E.activation(out=st_[:], in_=wt[:], func=AF.Sign),
                       reads=("wt",), writes=("st_",))
                p.step("vector", lambda E, which=which: E.tensor_scalar(
                    mt[:], at[:], thr_b[:, which:which + 1], None, op0=ALU.is_ge),
                       reads=("at", "thr_b"), writes=("mt",))
                p.step("vector", lambda E: E.tensor_mul(qt[:], st_[:], mt[:]),
                       reads=("st_", "mt"), writes=("qt",))
                p.dma("gpsimd", dap, qt[:], reads=("qt",), writes=("o",))
        p.build(st)
    return nc


def stage_I(inputs, core):
    w1 = inputs["L1_ffn_w1"]
    w2 = inputs["L1_ffn_w2"]
    return {"w1s": np.ascontiguousarray(w1[core * 1024:(core + 1) * 1024]),
            "w2s": np.ascontiguousarray(w2[core * 256:(core + 1) * 256])}


def build_J():
    nc = make_nc()
    i_x3 = nc.dram_tensor("x3", [DM, TPC], F32, kind="ExternalInput").ap()
    i_w1t = nc.dram_tensor("w1q_t", [DM, 8192], BF16, kind="ExternalInput").ap()
    i_b1 = nc.dram_tensor("b1", [128, 64], F32, kind="ExternalInput").ap()
    i_w2t = nc.dram_tensor("w2q_t", [8192, DM], BF16, kind="ExternalInput").ap()
    i_b2 = nc.dram_tensor("b2", [128, 16], F32, kind="ExternalInput").ap()
    o_out = nc.dram_tensor("out", [DM, TPC], F32, kind="ExternalOutput").ap()
    h_d = nc.dram_tensor("h_d", [8192, TPC], BF16, kind="Internal").ap()
    rows_d = nc.dram_tensor("rows_d", [2, TPC], F32, kind="Internal").ap()

    p = P(nc)
    with ExitStack() as st:
        e = st.enter_context
        xbuf = [e(nc.sbuf_tensor(f"xbuf{i}", [128, TPC], F32)) for i in range(2)]
        sqt = e(nc.sbuf_tensor("sqt", [128, TPC], F32))
        ones1 = e(nc.sbuf_tensor("ones1", [128, 1], F32))
        consts = e(nc.sbuf_tensor("consts", [128, 4], F32))
        xn_sb = e(nc.sbuf_tensor("xn_sb", [128, 16 * TPC], BF16))
        mu_b = e(nc.sbuf_tensor("mu_b", [128, TPC], F32))
        rstd_b = e(nc.sbuf_tensor("rstd_b", [128, TPC], F32))
        wt = [e(nc.sbuf_tensor(f"wt{i}", [128, 2048], BF16)) for i in range(2)]
        hbf = e(nc.sbuf_tensor("hbf", [128, TPC], BF16))
        b1_sb = e(nc.sbuf_tensor("b1_sb", [128, 64], F32))
        b2_sb = e(nc.sbuf_tensor("b2_sb", [128, 16], F32))
        wo = [e(nc.sbuf_tensor(f"wo{i}", [128, 8192], BF16)) for i in range(4)]
        ain = [e(nc.sbuf_tensor(f"ain{i}", [128, TB], BF16)) for i in range(2)]
        xin = e(nc.sbuf_tensor("xin", [128, TB], F32))
        x4t = e(nc.sbuf_tensor("x4t", [128, TB], F32))
        ps_a = e(nc.psum_tensor("ps_a", [128, 2048], F32))
        ps_y = [e(nc.psum_tensor(f"ps_y{i}", [128, TB], F32)) for i in range(4)]

        p.step("vector", lambda E: E.memset(ones1[:], 1.0), writes=("ones1",))
        p.step("vector", lambda E: E.memset(consts[:, 0:1], EPS), writes=("consts",))
        p.dma("sync", b1_sb[:], i_b1[:], writes=("b1",))
        p.dma("sync", b2_sb[:], i_b2[:], writes=("b2",))
        for dc in range(16):
            b = dc % 2
            p.dma("sync", xbuf[b][:], i_x3[dc * 128:(dc + 1) * 128, :], writes=(f"xbuf{b}",))
            def mmrow(E, dc=dc, b=b):
                last = None
                for (o, n) in NT_FULL:
                    last = E.matmul(ps_a[0:1, o:o + n], ones1[:], xbuf[b][:, o:o + n],
                                    start=(dc == 0), stop=(dc == 15))
                return last
            p.step("tensor", mmrow, reads=(f"xbuf{b}", "ones1"), writes=("ps_a",))
        p.step("vector", lambda E: E.tensor_scalar_mul(mu_b[0:1, :], ps_a[0:1, 0:TPC], 1.0 / DM),
               reads=("ps_a",), writes=("murow",))
        for dc in range(16):
            b = dc % 2
            p.dma("sync", xbuf[b][:], i_x3[dc * 128:(dc + 1) * 128, :], writes=(f"xbuf{b}",))
            p.step("scalar", lambda E, b=b: E.activation(out=sqt[:], in_=xbuf[b][:], func=AF.Square),
                   reads=(f"xbuf{b}",), writes=("sqt",))
            def mmrow2(E, dc=dc, b=b):
                last = None
                for (o, n) in NT_FULL:
                    last = E.matmul(ps_a[0:1, o:o + n], ones1[:], sqt[:, o:o + n],
                                    start=(dc == 0), stop=(dc == 15))
                return last
            p.step("tensor", mmrow2, reads=("sqt", "ones1"), writes=("ps_a",))
        p.step("vector", lambda E: E.tensor_scalar_mul(rstd_b[0:1, :], ps_a[0:1, 0:TPC], 1.0 / DM),
               reads=("ps_a",), writes=("varow",))
        p.step("vector", lambda E: E.tensor_mul(sqt[0:1, 0:TPC], mu_b[0:1, :], mu_b[0:1, :]),
               reads=("murow",), writes=("sqrow",))
        p.step("vector", lambda E: E.tensor_sub(rstd_b[0:1, :], rstd_b[0:1, :], sqt[0:1, 0:TPC]),
               reads=("sqrow", "varow"), writes=("varow",))
        p.step("scalar", lambda E: E.activation(out=rstd_b[0:1, :], in_=rstd_b[0:1, :],
                                                func=AF.Sqrt, bias=consts[0:1, 0:1]),
               reads=("varow", "consts"), writes=("varow",))
        p.step("vector", lambda E: E.reciprocal(rstd_b[0:1, :], rstd_b[0:1, :]),
               reads=("varow",), writes=("varow",))
        p.dma("gpsimd", rows_d[0:1, :], mu_b[0:1, :], reads=("murow",), writes=("rows_d",))
        p.dma("gpsimd", rows_d[1:2, :], rstd_b[0:1, :], reads=("varow",), writes=("rows_d",))
        p.dma("sync", mu_b[:], rows_d[0:1, :].partition_broadcast(128),
              reads=("rows_d",), writes=("mu_b",))
        p.dma("sync", rstd_b[:], rows_d[1:2, :].partition_broadcast(128),
              reads=("rows_d",), writes=("rstd_b",))
        for dc in range(16):
            b = dc % 2
            p.dma("sync", xbuf[b][:], i_x3[dc * 128:(dc + 1) * 128, :], writes=(f"xbuf{b}",))
            p.step("vector", lambda E, b=b: E.tensor_sub(sqt[:], xbuf[b][:], mu_b[:]),
                   reads=(f"xbuf{b}", "mu_b"), writes=("sqt",))
            p.step("vector", lambda E, dc=dc: E.tensor_mul(
                xn_sb[:, dc * TPC:(dc + 1) * TPC], sqt[:], rstd_b[:]),
                   reads=("sqt", "rstd_b"), writes=("xn",))
        for fc in range(64):
            b = fc % 2
            src = bass.AP(i_w1t.tensor, fc * 128, [[8192, 128], [128 * 8192, 16], [1, 128]])
            p.dma("gpsimd", wt[b][:].rearrange("p (a j) -> p a j", a=16), src,
                  writes=(f"wt{b}",))
            def mmh(E, b=b):
                last = None
                for dc in range(16):
                    for (o, n) in NT_FULL:
                        last = E.matmul(ps_a[:, o:o + n], wt[b][:, dc * 128:(dc + 1) * 128],
                                        xn_sb[:, dc * TPC + o: dc * TPC + o + n],
                                        start=(dc == 0), stop=(dc == 15))
                return last
            p.step("tensor", mmh, reads=(f"wt{b}", "xn"), writes=("ps_a",))
            p.step("scalar", lambda E, fc=fc: E.activation(
                out=hbf[:], in_=ps_a[:, 0:TPC], func=AF.Gelu, bias=b1_sb[:, fc:fc + 1]),
                   reads=("ps_a", "b1"), writes=("hbf",))
            p.dma("gpsimd", h_d[fc * 128:(fc + 1) * 128, :], hbf[:],
                  reads=("hbf",), writes=("h_d",))
        for grp, g0, gn in ((0, 0, 4), (1, 4, 4), (2, 8, 4), (3, 12, 4)):
            for dm in range(gn):
                dmg = g0 + dm
                src = bass.AP(i_w2t.tensor, dmg * 128, [[DM, 128], [128 * DM, 64], [1, 128]])
                p.dma("gpsimd", wo[dm][:].rearrange("p (a j) -> p a j", a=64), src,
                      writes=(f"wo{dm}",))
            for tb4 in range(NTB):
                t0 = tb4 * TB
                for c in range(64):
                    b = c % 2
                    p.dma("sync", ain[b][:], h_d[c * 128:(c + 1) * 128, t0:t0 + TB],
                          reads=("h_d",), writes=(f"ain{b}",))
                    def mmout(E, b=b, c=c, gn=gn):
                        last = None
                        for dm in range(gn):
                            last = E.matmul(ps_y[dm][:], wo[dm][:, c * 128:(c + 1) * 128],
                                            ain[b][:], start=(c == 0), stop=(c == 63))
                        return last
                    p.step("tensor", mmout,
                           reads=(f"ain{b}",) + tuple(f"wo{dm}" for dm in range(gn)),
                           writes=tuple(f"ps{dm}" for dm in range(gn)))
                for dm in range(gn):
                    dmg = g0 + dm
                    p.dma("sync", xin[:], i_x3[dmg * 128:(dmg + 1) * 128, t0:t0 + TB],
                          writes=("xin",))
                    p.step("vector", lambda E, dm=dm, dmg=dmg: E.tensor_scalar(
                        x4t[:], ps_y[dm][:], b2_sb[:, dmg:dmg + 1], None, op0=ALU.add),
                           reads=(f"ps{dm}", "b2"), writes=("x4t",))
                    p.step("vector", lambda E: E.tensor_add(x4t[:], x4t[:], xin[:]),
                           reads=("x4t", "xin"), writes=("x4t",))
                    p.dma("gpsimd", o_out[dmg * 128:(dmg + 1) * 128, t0:t0 + TB], x4t[:],
                          reads=("x4t",), writes=("o_out",))
        p.build(st)
    return nc


def stage_J(inputs, core, x3, w1q, w2q):
    return {"x3": x3[core],
            "w1q_t": np.ascontiguousarray(w1q.T),
            "b1": np.ascontiguousarray(inputs["L1_ffn_b1"].reshape(64, 128).T),
            "w2q_t": np.ascontiguousarray(w2q.T),
            "b2": np.ascontiguousarray(inputs["L1_ffn_b2"].reshape(16, 128).T)}


from concourse.bass_utils import run_bass_kernel_spmd
import ml_dtypes


def _run(nc, maps):
    return run_bass_kernel_spmd(nc, maps, core_ids=list(range(8))).results


def kernel(**inputs):
    inputs = {k: np.asarray(v) for k, v in inputs.items()}
    rA = _run(build_A(), [stage_A(inputs, c) for c in range(8)])
    prev = {k: [rA[c][k] for c in range(8)] for k in ["xc", "delta", "z", "bc", "dtot"]}
    rB1 = _run(build_B1(), [stage_B1(inputs, c, prev) for c in range(8)])
    prev["hend"] = [rB1[c]["hend"] for c in range(8)]
    rB2 = _run(build_B2(), [stage_B2(inputs, c, prev) for c in range(8)])
    x1 = [rB2[c]["x1"] for c in range(8)]
    rC = _run(build_C(), [stage_C(inputs, c, x1) for c in range(8)])
    moeC = {k: [rC[c][k] for c in range(8)] for k in
            ["x2n_tm", "shared", "e_idx", "wnorm", "ctot"]}
    rD = _run(build_D(), [stage_D(inputs, c, moeC) for c in range(8)])
    moeD = {"partial": [rD[c]["partial"] for c in range(8)]}
    rE = _run(build_E(), [stage_E(inputs, c, moeC, moeD) for c in range(8)])
    pout = [rE[c]["pout"] for c in range(8)]
    rF = _run(build_F(), [stage_F(inputs, c, x1, moeC, pout) for c in range(8)])
    x2 = [rF[c]["x2"] for c in range(8)]
    rG = _run(build_G(), [stage_G(inputs, c, x2) for c in range(8)])
    attnG = {k: [rG[c][k] for c in range(8)] for k in ["attn_tm", "q_fm", "s_end"]}
    rH = _run(build_H(), [stage_H(inputs, c, x2, attnG) for c in range(8)])
    x3 = [rH[c]["x3"] for c in range(8)]
    rI1 = _run(build_I1(), [stage_I(inputs, c) for c in range(8)])
    parts = np.array([rI1[c]["parts"][0] for c in range(8)])
    prow = np.concatenate([parts[:, 0], parts[:, 1]]).reshape(1, 16).astype(np.float32)
    rI2 = _run(build_I2(), [dict(stage_I(inputs, c), parts=prow) for c in range(8)])
    w1q = np.concatenate([rI2[c]["w1q"] for c in range(8)], 0)
    w2q = np.concatenate([rI2[c]["w2q"] for c in range(8)], 0)
    rJ = _run(build_J(), [stage_J(inputs, c, x3, w1q, w2q) for c in range(8)])
    out = np.zeros((2, 8192, 2048), np.float32)
    for c in range(8):
        b, q = c // 4, c % 4
        out[b, q * TPC:(q + 1) * TPC] = rJ[c]["out"].T
    return out
